# version 18
# baseline (speedup 1.0000x reference)
import numpy as np
import ml_dtypes
import concourse.bass as bass
import concourse.mybir as mybir
from concourse.bass_utils import run_bass_kernel_spmd

F32 = mybir.dt.float32
BF16 = mybir.dt.bfloat16
ALU = mybir.AluOpType
AF = mybir.ActivationFunctionType
AX = mybir.AxisListType
BFNP = ml_dtypes.bfloat16

D = 1024
IFFN = 2816
NBLK = 32
BPS = 16
NSPAN = 4
C05 = float(3 * 2**22) * 0.5
C1F = float(3 * 2**22)
C2F = float(3 * 2**23)
SCL_B = float(64 ** -0.25)
EPS_OUT = 1e-5
EPS_IN = 1e-3
EPS_QK = 1e-6
NCHUNKS = [(j * 512, min(512, IFFN - j * 512)) for j in range((IFFN + 511) // 512)]

ENGS = ("sync", "gpsimd", "vector", "scalar", "tensor")
DMAE = {"sync", "gpsimd"}


def bcast_inner(ap, n):
    return bass.AP(ap.tensor, ap.offset, [list(x) for x in ap.ap] + [[0, n]])


class Prog:
    def __init__(self, nc):
        self.nc = nc
        self.ops = {e: [] for e in ENGS}
        self.reg = {}
        self.clock = {e: {e2: -1 for e2 in ENGS} for e in ENGS}
        self.scope = "init"

    def emit(self, eng, builder, reads=(), writes=(), dma=False, tag=""):
        idx = len(self.ops[eng])
        waits = {}

        def add_wait(e2, i2):
            if self.clock[eng][e2] >= i2:
                return
            if waits.get(e2, -1) < i2:
                waits[e2] = i2

        for k in reads:
            r = self.reg.get(k)
            if r and r["w"] is not None:
                add_wait(*r["w"])
        for k in writes:
            r = self.reg.get(k)
            if r:
                if r["w"] is not None and r["w"][0] != eng:
                    add_wait(*r["w"])
                for e2, i2 in r["r"].items():
                    if e2 != eng:
                        add_wait(e2, i2)
        for e2, i2 in waits.items():
            self.ops[e2][i2]["inc"] = True
            if self.clock[eng][e2] < i2:
                self.clock[eng][e2] = i2
        self.ops[eng].append({"b": builder, "waits": waits, "inc": dma,
                              "dma": dma, "tag": tag, "sc": self.scope})
        for k in reads:
            r = self.reg.setdefault(k, {"w": None, "r": {}})
            r["r"][eng] = idx
        for k in writes:
            self.reg[k] = {"w": (eng, idx), "r": {}}
        return (eng, idx)

    def barrier_wait(self, eng, handles):
        waits = {}
        for (e2, i2) in handles:
            self.ops[e2][i2]["inc"] = True
            if waits.get(e2, -1) < i2:
                waits[e2] = i2
        self.ops[eng].append({"b": None, "waits": waits, "inc": False,
                              "dma": False, "tag": "barrier", "sc": self.scope})

    def finalize(self, sems):
        runend = {}
        for e in DMAE:
            ops = self.ops[e]
            n = len(ops)
            arr = [0] * n
            end = n - 1
            for i in range(n - 1, -1, -1):
                op = ops[i]
                if not op["dma"]:
                    end = i - 1 if i > 0 else -1
                    arr[i] = i
                    continue
                arr[i] = end if end >= i else i
                if op["waits"] or op["b"] is None:
                    end = i - 1
            runend[e] = arr
        for e in ENGS:
            for op in self.ops[e]:
                nw = {}
                for e2, i2 in op["waits"].items():
                    if e2 in DMAE and self.ops[e2][i2]["dma"]:
                        j = runend[e2][i2]
                        if j > i2:
                            self.ops[e2][j]["inc"] = True
                            i2 = j
                    nw[e2] = max(nw.get(e2, -1), i2)
                op["waits"] = nw
        prefix = {}
        for e in ENGS:
            c = 0
            arr = []
            for op in self.ops[e]:
                if op["inc"]:
                    c += 16 if op["dma"] else 1
                arr.append(c)
            prefix[e] = arr
        stats = {e: (len(self.ops[e]), prefix[e][-1] if self.ops[e] else 0)
                 for e in ENGS}

        def mk_body(ename):
            def body(eng):
                sem_own = sems[ename]
                cur = [None, None]
                for op in self.ops[ename]:
                    if op["sc"] != cur[0]:
                        if cur[0] is not None:
                            self.nc.leave_named_scope(cur[0], cur[1], False)
                        sid, _ = self.nc.enter_named_scope(op["sc"], False)
                        cur[0], cur[1] = op["sc"], sid
                    for e2 in sorted(op["waits"]):
                        i2 = op["waits"][e2]
                        eng.wait_ge(sems[e2], prefix[e2][i2])
                    if op["b"] is None:
                        continue
                    ins = op["b"](eng)
                    if op["inc"]:
                        ins.then_inc(sem_own, 16 if op["dma"] else 1)
                if cur[0] is not None:
                    self.nc.leave_named_scope(cur[0], cur[1], False)
            return body
        return mk_body, stats


class Env:
    def __init__(self, p, d):
        self.p = p
        for k, v in d.items():
            setattr(self, k, v)
        self.scr_rot = 2
        self.scr_lo = 2
        self.q_rot = 0
        self.qpend = []

    def V(self, builder, r=(), w=(), tag=""):
        return self.p.emit("vector", builder, r, w, tag=tag)

    def S(self, builder, r=(), w=(), tag=""):
        return self.p.emit("scalar", builder, r, w, tag=tag)

    def T(self, builder, r=(), w=(), tag=""):
        return self.p.emit("tensor", builder, r, w, tag=tag)

    def DMA(self, q, builder, r=(), w=(), tag=""):
        return self.p.emit(q, builder, r, w, dma=True, tag=tag)

    def P(self, builder, r=(), w=(), tag=""):
        return self.p.emit("gpsimd", builder, r, w, tag=tag)


def wreg_keys(slot, c0, c1, h0, h1):
    return [("wreg", slot, c, h) for c in range(c0, c1) for h in range(h0, h1)]


def vpk(col0, col1):
    return [("vpk", b) for b in range(col0 // 512, (col1 + 511) // 512)]


def wbk(col0, col1):
    return [("wbk", b) for b in range(col0 // 512, (col1 + 511) // 512)]


def qpush(e, bk):
    e.qpend.append(bk)
    if len(e.qpend) > 1:
        e.qpend.pop(0)()


def qflush(e):
    while e.qpend:
        e.qpend.pop(0)()


def scr_slot(e, n=1):
    hi = 8
    if e.scr_rot < e.scr_lo or e.scr_rot + n > hi:
        e.scr_rot = e.scr_lo
    s = e.scr_rot
    e.scr_rot += n
    return e.scr[:, s * 512:(s + n) * 512], [("scr", s + i) for i in range(n)]


def dma_slot(e, n=2):
    return e.scr[:, 0:n * 512], [("scr", i) for i in range(n)]


def qsc(e, nb):
    s = e.q_rot
    e.q_rot ^= 1
    base = 48 + s * 128
    return (e.smf[:, base:base + nb], ("scs", s),
            e.smf[:, base + 64:base + 64 + nb], ("rss", s))



def build_program():
    nc = bass.Bass("TRN2", target_bir_lowering=False, debug=False)
    p = Prog(nc)

    xin = nc.dram_tensor("xin", [NBLK * 128, D], F32, kind="ExternalInput").ap()
    out = nc.dram_tensor("out", [NBLK * 128, D], F32, kind="ExternalOutput").ap()
    W = {}
    for n in ["Wq", "Wk", "Wv", "Wq2", "Wk2", "Wv2", "Wo"]:
        W["a_" + n] = nc.dram_tensor("L0_attn_a_" + n, [D, D], F32, kind="ExternalInput").ap()
    for n in ["bq", "bk", "bv", "bq2", "bk2", "bv2", "bo", "gamma", "beta"]:
        W["a_" + n] = nc.dram_tensor("L0_attn_a_" + n, [D], F32, kind="ExternalInput").ap()
    for n in ["Wq", "Wk", "Wv", "Wo"]:
        W["b_" + n] = nc.dram_tensor("L0_attn_b_" + n, [D, D], F32, kind="ExternalInput").ap()
    for n in ["bq", "bv", "bo"]:
        W["b_" + n] = nc.dram_tensor("L0_attn_b_" + n, [D], F32, kind="ExternalInput").ap()
    W["f_wg"] = nc.dram_tensor("L0_ffn_w_gate", [IFFN, D], F32, kind="ExternalInput").ap()
    W["f_wu"] = nc.dram_tensor("L0_ffn_w_up", [IFFN, D], F32, kind="ExternalInput").ap()
    W["f_wd"] = nc.dram_tensor("L0_ffn_w_down", [D, IFFN], F32, kind="ExternalInput").ap()

    wgq = nc.dram_tensor("wgq_scr", [D, IFFN], BF16, kind="Internal").ap()
    wuq = nc.dram_tensor("wuq_scr", [D, IFFN], BF16, kind="Internal").ap()
    wdq = nc.dram_tensor("wdq_scr", [IFFN, D], BF16, kind="Internal").ap()

    ident_d = nc.inline_tensor(np.eye(128, dtype=BFNP), name="ident_bf")
    ones_d = nc.inline_tensor(np.ones((1, 128), dtype=BFNP), name="ones_bf")
    cvals = np.zeros((128, 16), dtype=np.float32)
    cvals[:, 1] = 1.0
    cvals[:, 2] = EPS_OUT
    cvals[:, 3] = EPS_IN
    cvals[:, 4] = EPS_QK
    cvals[:, 5] = C1F
    cvals[:, 6] = -C1F
    cvals[:, 7] = C2F
    cvals[:, 8] = -C2F
    cpk_d = nc.inline_tensor(cvals, name="cpack")

    import contextlib
    es = contextlib.ExitStack()
    sb = lambda name, shape, dt: es.enter_context(nc.sbuf_tensor(name, shape, dt))
    psA = lambda name, shape, dt: es.enter_context(nc.psum_tensor(name, shape, dt))
    with es:
        stream = sb("stream", [128, NBLK * 1024], BF16)
        wreg = sb("wreg", [128, 3 * 8192], BF16)
        vprime = sb("vprime", [128, 16384], BF16)
        wbuf = sb("wbuf", [128, 10240], BF16)
        scr = sb("scr", [128, 4096], F32)
        lnt = sb("lnt", [128, 2048], BF16)
        lnf = sb("lnf", [128, 1024], BF16)
        q2t = sb("q2t", [128, 1024], BF16)
        k2t = sb("k2t", [128, 1024], BF16)
        sbd = sb("sbd", [128, 1024], BF16)
        btile = sb("btile", [128, 6144], BF16)
        sta = sb("statst", [128, 128], F32)
        smf = sb("smf", [128, 384], F32)
        misc = sb("miscb", [128, 512], BF16)
        cpk = sb("cpk", [128, 16], F32)

        pool = [psA(f"pool{i}", [128, 512], F32) for i in range(4)]
        acc = psA("acc", [128, 1024], F32)
        tpA = psA("tpA", [128, 1024], BF16)
        tpB = psA("tpB", [128, 1024], BF16)

        sems = {}
        for en in ENGS:
            sems[en] = es.enter_context(nc.semaphore("sem_" + en))

        nc.const_aps.register(F32, 0.0, cpk[:, 0:1])
        nc.const_aps.register(F32, 1.0, cpk[:, 1:2])
        nc.const_aps.register(F32, EPS_OUT, cpk[:, 2:3])
        nc.const_aps.register(F32, EPS_IN, cpk[:, 3:4])
        nc.const_aps.register(F32, EPS_QK, cpk[:, 4:5])
        nc.const_aps.register(F32, C1F, cpk[:, 5:6])
        nc.const_aps.register(F32, -C1F, cpk[:, 6:7])
        nc.const_aps.register(F32, C2F, cpk[:, 7:8])
        nc.const_aps.register(F32, -C2F, cpk[:, 8:9])

        env = dict(p=p, nc=nc, xin=xin, out=out, W=W, wgq=wgq, wuq=wuq,
                   wdq=wdq, ident_d=ident_d, ones_d=ones_d, cpk_d=cpk_d,
                   stream=stream, wreg=wreg, vprime=vprime, wbuf=wbuf,
                   scr=scr, lnt=lnt, lnf=lnf, q2t=q2t, k2t=k2t, sbd=sbd,
                   btile=btile, sta=sta, smf=smf, misc=misc, cpk=cpk,
                   pool=pool, acc=acc, tpA=tpA, tpB=tpB)
        e = Env(p, env)
        build_body(e)
        mk_body, stats = p.finalize(sems)

        block = es.enter_context(nc.Block())
        block.sync(mk_body("sync"))
        block.gpsimd(mk_body("gpsimd"))
        block.vector(mk_body("vector"))
        block.scalar(mk_body("scalar"))
        block.tensor(mk_body("tensor"))

    import os
    if os.environ.get("KDEBUG"):
        print("program op counts (n_ops, n_incs):", stats)
    return nc


def build_body(e):
    p = e.p
    h_id = e.DMA("sync", lambda g: g.dma_start(out=e.misc[:, 0:128], in_=e.ident_d.ap()),
                 w=[("misc", "id")])
    h_on = e.DMA("sync", lambda g: g.dma_start(out=e.misc[0:1, 128:256], in_=e.ones_d.ap()),
                 w=[("misc", "ones")])
    h_cp = e.DMA("sync", lambda g: g.dma_start(out=e.cpk[:], in_=e.cpk_d.ap()),
                 w=[("cpk",)])
    p.barrier_wait("scalar", [h_id, h_on, h_cp])
    p.barrier_wait("vector", [h_id, h_on, h_cp])
    p.barrier_wait("tensor", [h_id, h_on, h_cp])

    import os
    ph = int(os.environ.get("KPHASES", "7"))
    p.scope = "loadx"
    load_x(e)
    if ph >= 2:
        p.scope = "prepA"
        prep_attn_a_weights(e)
    if ph >= 3:
        p.scope = "prepF"
        prep_ffn_weights(e)
    if ph >= 4:
        p.scope = "stats1"
        stats_agg(e)
        p.scope = "attnA"
        attn_a(e)
    if ph >= 5:
        p.scope = "stats2"
        stats_agg(e)
        p.scope = "prepBkv"
        prep_attn_b_kv_weights(e)
        p.scope = "ffn1"
        ffn(e, final=False)
    if ph >= 6:
        p.scope = "stats3"
        stats_agg(e)
        p.scope = "prepBqo"
        prep_attn_b_qo_weights(e)
        p.scope = "attnB"
        attn_b(e)
    if ph >= 7:
        p.scope = "stats4"
        stats_agg(e)
        p.scope = "ffn2"
        ffn(e, final=True)

    if p.ops["gpsimd"]:
        last = len(p.ops["gpsimd"]) - 1
        p.barrier_wait("gpsimd", [("gpsimd", last)])
    if p.ops["sync"]:
        p.barrier_wait("sync", [("sync", len(p.ops["sync"]) - 1)])


def load_x(e):
    for b in range(NBLK):
        st_ap, st_k = dma_slot(e, 2)
        e.DMA("sync", lambda g, b=b, st_ap=st_ap: g.dma_start(
            out=st_ap, in_=e.xin[b * 128:(b + 1) * 128, :]),
            w=st_k, tag=f"ldx{b}")
        e.V(lambda v, b=b, st_ap=st_ap: v.tensor_copy(
            e.stream[:, b * 1024:(b + 1) * 1024], st_ap),
            r=st_k, w=[("st", b)], tag=f"castx{b}")
        stats_block(e, b)



def transpose_1024(e, src_ap, src_keys, dst3d_ap, dst_keys, tp, tpn, tag="tp",
                   nblocks=8):
    for c in range(nblocks):
        e.T(lambda t, c=c, tp=tp, src_ap=src_ap: t.transpose(
            tp[:, c * 128:(c + 1) * 128], src_ap[:, c * 128:(c + 1) * 128],
            e.misc[:, 0:128]),
            r=list(src_keys) + [("misc", "id")], w=[(tpn,)], tag=f"{tag}t{c}")
    e.S(lambda s, tp=tp, dst3d_ap=dst3d_ap, nblocks=nblocks: s.copy(
        dst3d_ap, tp[:, 0:nblocks * 128].rearrange("p (c x) -> p c x", x=128)),
        r=[(tpn,)], w=list(dst_keys), tag=f"{tag}cp")


def bias_tile_from_row(e, rowbf_ap, row_keys, bslot, tag="bc"):
    for h in range(2):
        e.T(lambda t, h=h, rowbf_ap=rowbf_ap: t.matmul(
            e.pool[3][:, 0:512], e.misc[0:1, 128:256],
            rowbf_ap[0:1, h * 512:(h + 1) * 512], start=True, stop=True),
            r=list(row_keys) + [("misc", "ones")], w=[("pool", 3)], tag=f"{tag}mm{h}")
        e.S(lambda s, h=h, bslot=bslot: s.copy(
            e.btile[:, bslot * 1024 + h * 512:bslot * 1024 + (h + 1) * 512],
            e.pool[3][:, 0:512]),
            r=[("pool", 3)], w=[("bt", bslot, h)], tag=f"{tag}cp{h}")


def prep_direct_bias(e, vec_name, bslot, scale=None, tag="db"):
    rf_ap, rf_k = dma_slot(e, 2)
    rowf = rf_ap[0:1, 0:1024]
    e.DMA("sync", lambda g, rowf=rowf, vec_name=vec_name: g.dma_start(
        out=rowf, in_=e.W[vec_name].rearrange("(a b) -> a b", a=1)),
        w=rf_k, tag=f"{tag}ld")
    rowb = e.wbuf[0:1, 9216:10240]
    rowb_k = wbk(9216, 10240)
    if scale is None:
        e.V(lambda v, rowf=rowf, rowb=rowb: v.tensor_copy(rowb, rowf),
            r=rf_k, w=rowb_k, tag=f"{tag}c")
    else:
        e.V(lambda v, rowf=rowf, rowb=rowb, scale=scale: v.tensor_scalar(
            rowb, rowf, scale, None, ALU.mult),
            r=rf_k, w=rowb_k, tag=f"{tag}c")
    bias_tile_from_row(e, rowb, rowb_k, bslot, tag=tag)


def prep_attn_a_weights(e):
    for r in range(8):
        st_ap, st_k = dma_slot(e, 2)
        e.DMA("sync", lambda g, r=r, st_ap=st_ap: g.dma_start(
            out=st_ap, in_=e.W["a_Wo"][r * 128:(r + 1) * 128, :]),
            w=st_k, tag=f"wold{r}")
        e.V(lambda v, r=r, st_ap=st_ap: v.tensor_copy(
            e.wbuf[:, r * 1024:(r + 1) * 1024], st_ap),
            r=st_k, w=wbk(r * 1024, (r + 1) * 1024), tag=f"wocast{r}")
    for vec, slot in [("a_bo", 3), ("a_gamma", 4), ("a_beta", 5)]:
        prep_direct_bias(e, vec, slot, tag=f"db{vec}")
    for i, (w1, w2, b1, b2) in enumerate([
            ("a_Wq", "a_Wq2", "a_bq", "a_bq2"),
            ("a_Wk", "a_Wk2", "a_bk", "a_bk2"),
            ("a_Wv", "a_Wv2", "a_bv", "a_bv2")]):
        prep_fused_pair(e, w1, w2, b1, b2, i)


def prep_fused_pair(e, w1, w2, b1, b2, slot):
    for r in range(8):
        st_ap, st_k = dma_slot(e, 2)
        e.DMA("sync", lambda g, r=r, st_ap=st_ap, w2=w2: g.dma_start(
            out=st_ap, in_=e.W[w2][r * 128:(r + 1) * 128, :]),
            w=st_k, tag=f"{w2}ld{r}")
        e.V(lambda v, r=r, st_ap=st_ap: v.tensor_copy(
            e.vprime[:, r * 1024:(r + 1) * 1024], st_ap),
            r=st_k, w=vpk(r * 1024, (r + 1) * 1024), tag=f"{w2}c{r}")
    for kc in range(8):
        st_ap, st_k = dma_slot(e, 2)
        e.DMA("sync", lambda g, kc=kc, st_ap=st_ap, w1=w1: g.dma_start(
            out=st_ap, in_=e.W[w1][kc * 128:(kc + 1) * 128, :]),
            w=st_k, tag=f"{w1}ld{kc}")
        tmp = e.lnt[:, 0:1024]
        e.V(lambda v, st_ap=st_ap, tmp=tmp: v.tensor_copy(tmp, st_ap),
            r=st_k, w=[("lnt", 0), ("lnt", 1)], tag=f"{w1}c{kc}")
        dst3d = e.vprime[:, 8192:16384].rearrange(
            "p (m t) -> p m t", m=8)[:, :, kc * 128:(kc + 1) * 128]
        transpose_1024(e, tmp, [("lnt", 0), ("lnt", 1)], dst3d,
                       vpk(8192, 16384), e.tpA, "tpA", tag=f"{w1}T{kc}")
    for kc in range(8):
        for jh in range(2):
            bi = (kc * 2 + jh) % 3
            bank = e.pool[bi]
            for mc in range(8):
                e.T(lambda t, kc=kc, jh=jh, mc=mc, bank=bank: t.matmul(
                    bank[:],
                    e.vprime[:, 8192 + mc * 1024 + kc * 128:
                             8192 + mc * 1024 + (kc + 1) * 128],
                    e.vprime[:, mc * 1024 + jh * 512:mc * 1024 + (jh + 1) * 512],
                    start=(mc == 0), stop=(mc == 7)),
                    r=vpk(8192 + mc * 1024, 8192 + (mc + 1) * 1024) +
                    vpk(mc * 1024, (mc + 1) * 1024),
                    w=[("pool", bi)], tag=f"fus{slot}")
            e.S(lambda s, kc=kc, jh=jh, bank=bank, slot=slot, bi=bi: s.copy(
                e.wreg[:, slot * 8192 + kc * 1024 + jh * 512:
                       slot * 8192 + kc * 1024 + (jh + 1) * 512], bank[:]),
                r=[("pool", bi)], w=[("wreg", slot, kc, jh)], tag=f"fusc{slot}")
    rf_ap, rf_k = dma_slot(e, 2)
    rowf = rf_ap[0:1, 0:1024]
    e.DMA("sync", lambda g, rowf=rowf, b1=b1: g.dma_start(
        out=rowf, in_=e.W[b1].rearrange("(a b) -> a b", a=1)),
        w=rf_k, tag=f"ld{b1}")
    rowb = e.lnt[0:1, 0:1024]
    e.V(lambda v, rowf=rowf, rowb=rowb: v.tensor_copy(rowb, rowf),
        r=rf_k, w=[("lnt", 0), ("lnt", 1)], tag=f"c{b1}")
    for c in range(8):
        e.T(lambda t, c=c, rowb=rowb: t.transpose(
            e.tpA[:, 2 * c:2 * c + 1], rowb[0:1, c * 128:(c + 1) * 128],
            e.misc[0:1, 0:1]),
            r=[("lnt", 0), ("lnt", 1), ("misc", "id")], w=[("tpA",)], tag="bfT")
    e.S(lambda s: s.copy(
        e.k2t[:, 0:8].rearrange("p (c x) -> p c x", x=1),
        e.tpA[:, 0:16].rearrange("p (c x) -> p c x", x=2)[:, :, 0:1]),
        r=[("tpA",)], w=[("k2t", 0)], tag="bfTc")
    rf2_ap, rf2_k = scr_slot(e, 2)
    rowf2 = rf2_ap[0:1, 0:1024]
    e.DMA("sync", lambda g, rowf2=rowf2, b2=b2: g.dma_start(
        out=rowf2, in_=e.W[b2].rearrange("(a b) -> a b", a=1)),
        w=rf2_k, tag=f"ld{b2}")
    rowbb = e.lnf[0:1, 0:1024]
    for h in range(2):
        for mc in range(8):
            e.T(lambda t, h=h, mc=mc: t.matmul(
                e.pool[3][0:1, 0:512], e.k2t[:, mc:mc + 1],
                e.vprime[:, mc * 1024 + h * 512:mc * 1024 + (h + 1) * 512],
                start=(mc == 0), stop=(mc == 7)),
                r=[("k2t", 0)] + vpk(mc * 1024, (mc + 1) * 1024),
                w=[("pool", 3)], tag="bfmm")
        rs_ap, rs_k = scr_slot(e, 1)
        rsum = rs_ap[0:1, 0:512]
        e.V(lambda v, h=h, rsum=rsum, rowf2=rowf2: v.tensor_tensor(
            rsum, e.pool[3][0:1, 0:512], rowf2[0:1, h * 512:(h + 1) * 512], ALU.add),
            r=[("pool", 3)] + rf2_k, w=rs_k, tag="bfadd")
        e.V(lambda v, rsum=rsum, rowbb=rowbb: v.tensor_copy(rowbb[0:1, 0:512], rsum),
            r=rs_k, w=[("lnf", 0)], tag="bfcast")
        e.T(lambda t, rowbb=rowbb: t.matmul(
            e.pool[3][:, 0:512], e.misc[0:1, 128:256], rowbb[0:1, 0:512],
            start=True, stop=True),
            r=[("lnf", 0), ("misc", "ones")], w=[("pool", 3)], tag="bfbc")
        e.S(lambda s, h=h, slot=slot: s.copy(
            e.btile[:, slot * 1024 + h * 512:slot * 1024 + (h + 1) * 512],
            e.pool[3][:, 0:512]),
            r=[("pool", 3)], w=[("bt", slot, h)], tag="bfbcc")


def quant_window(e, src_ap, src_keys, w, dst_ap, dst_keys, bufs, tag="q"):
    sc_ap, sck, rs_ap, rsk = qsc(e, w // 16)
    src3 = src_ap.rearrange("p (b k) -> p b k", k=16)
    e.V(lambda v: v.tensor_reduce(sc_ap, src3, AX.X, ALU.max,
                                  apply_absolute_value=True),
        r=list(src_keys), w=[sck], tag=tag + "mx")
    e.V(lambda v: v.tensor_scalar(sc_ap, sc_ap, 1.0 / 6.0, 1e-12, ALU.mult, ALU.max),
        r=[sck], w=[sck], tag=tag + "sc")
    e.V(lambda v: v.reciprocal(rs_ap, sc_ap), r=[sck], w=[rsk], tag=tag + "rc")
    xn, xn_k = bufs["xn"]
    c1, c1_k = bufs["c1"]
    tb, tb_k = bufs["t"]
    e.V(lambda v: v.tensor_tensor(xn.rearrange("p (b k) -> p b k", k=16), src3,
                                  bcast_inner(rs_ap, 16), ALU.mult),
        r=list(src_keys) + [rsk], w=xn_k, tag=tag + "xn")
    e.V(lambda v: v.tensor_scalar(c1, xn, 2.0, -2.0, ALU.min, ALU.max),
        r=xn_k, w=c1_k, tag=tag + "c1")
    e.V(lambda v: v.tensor_tensor(tb, xn, c1, ALU.subtract),
        r=xn_k + c1_k, w=tb_k, tag=tag + "t")
    r1, r1_k = bufs["r1"]
    e.V(lambda v: v.tensor_scalar(r1, c1, C05, C05, ALU.add, ALU.subtract),
        r=c1_k, w=r1_k, tag=tag + "r1")
    e.V(lambda v: v.tensor_scalar(c1, tb, 2.0, -2.0, ALU.min, ALU.max),
        r=tb_k, w=c1_k, tag=tag + "d2")
    e.V(lambda v: v.tensor_tensor(xn, tb, c1, ALU.subtract),
        r=tb_k + c1_k, w=xn_k, tag=tag + "d3")
    r2, r2_k = bufs["r2"]
    e.V(lambda v: v.tensor_scalar(r2, c1, C1F, C1F, ALU.add, ALU.subtract),
        r=c1_k, w=r2_k, tag=tag + "r2")
    r3, r3_k = bufs["r3"]
    e.V(lambda v: v.tensor_scalar(r3, xn, C2F, C2F, ALU.add, ALU.subtract),
        r=xn_k, w=r3_k, tag=tag + "r3")
    g1, g1_k = bufs["g1"]
    e.V(lambda v: v.tensor_tensor(g1, r1, r2, ALU.add),
        r=r1_k + r2_k, w=g1_k, tag=tag + "g1")
    g, g_k = bufs["g"]
    e.V(lambda v: v.tensor_tensor(g, g1, r3, ALU.add),
        r=g1_k + r3_k, w=g_k, tag=tag + "g")
    e.V(lambda v: v.tensor_tensor(dst_ap.rearrange("p (b k) -> p b k", k=16),
                                  g.rearrange("p (b k) -> p b k", k=16),
                                  bcast_inner(sc_ap, 16), ALU.mult),
        r=g_k + [sck], w=list(dst_keys), tag=tag + "qq")


def bf16_bufs(e, w, par=0):
    if w > 512:
        return {
            "r1": (e.q2t[:, 0:w], [("q2t", 0), ("q2t", 1)]),
            "r2": (e.k2t[:, 0:w], [("k2t", 0), ("k2t", 1)]),
            "r3": (e.lnf[:, 0:w], [("lnf", 0), ("lnf", 1)]),
            "g1": (e.sbd[:, 0:w], [("sbd", i) for i in range(8)]),
            "g": (e.lnt[:, 1024:1024 + w], [("lnt", 2), ("lnt", 3)]),
        }
    o = par * 512
    return {
        "r1": (e.q2t[:, o:o + w], [("q2t", par)]),
        "r2": (e.k2t[:, o:o + w], [("k2t", par)]),
        "r3": (e.lnf[:, o:o + w], [("lnf", par)]),
        "g1": (e.sbd[:, o:o + w], [("sbd", i + 4 * par) for i in range(4)]),
        "g": (e.lnt[:, 1024 + o:1024 + o + w], [("lnt", 2 + par)]),
    }


def prep_ffn_weights(e):
    import os
    nwg = int(os.environ.get("KNWG", "22"))
    nwu = int(os.environ.get("KNWU", "22"))
    nwd = int(os.environ.get("KNWD", "8"))
    for r in range(nwg):
        prep_ffn_wgwu_tile(e, "wg", e.W["f_wg"], e.wgq, r)
    for r in range(nwu):
        prep_ffn_wgwu_tile(e, "wu", e.W["f_wu"], e.wuq, r)
    for r in range(nwd):
        prep_ffn_wd_tile(e, r)


PREPQ = [0]


def _quant_and_store_windows(e, wname, dram_dst, r, st_ap, base_col, nwcols,
                             stage_keys):
    nwin = (nwcols + 511) // 512
    for hw in range(nwin):
        w0 = hw * 512
        ww = min(nwcols, w0 + 512) - w0
        sw = st_ap[:, w0:w0 + ww]
        swk = [stage_keys[i] for i in range(w0 // 512, (w0 + ww + 511) // 512)]
        xn, xn_k = scr_slot(e, 1)
        c1a, c1_k = scr_slot(e, 1)
        ta, t_k = scr_slot(e, 1)
        PREPQ[0] ^= 1
        bufs = bf16_bufs(e, ww, PREPQ[0])
        bufs["xn"] = (xn[:, 0:ww], xn_k)
        bufs["c1"] = (c1a[:, 0:ww], c1_k)
        bufs["t"] = (ta[:, 0:ww], t_k)
        par = PREPQ[0]
        dst_bf = e.lnt[:, par * 512:par * 512 + ww]
        quant_window(e, sw, swk, ww, dst_bf, [("lnt", par)], bufs,
                     tag=f"{wname}q{r}_{base_col + w0}")
        for ci in range(ww // 128):
            c = (base_col + w0) // 128 + ci
            tp, tpn = (e.tpA, "tpA") if (c % 2 == 0) else (e.tpB, "tpB")
            sl = c % 8
            e.T(lambda t, ci=ci, tp=tp, sl=sl, dst_bf=dst_bf: t.transpose(
                tp[:, sl * 128:(sl + 1) * 128], dst_bf[:, ci * 128:(ci + 1) * 128],
                e.misc[:, 0:128]),
                r=[("lnt", par), ("misc", "id")], w=[("tpA",) if tp is e.tpA else ("tpB",)], tag=f"{wname}T")
            stg = e.misc[:, 256 + (c % 2) * 128:256 + (c % 2) * 128 + 128]
            e.S(lambda s, tp=tp, sl=sl, stg=stg: s.copy(
                stg, tp[:, sl * 128:(sl + 1) * 128]),
                r=[("tpA",) if tp is e.tpA else ("tpB",)], w=[("wstg", c % 2)], tag=f"{wname}Tc")
            e.DMA("gpsimd", lambda g, c=c, r=r, stg=stg, dram_dst=dram_dst:
                  g.dma_start(out=dram_dst[c * 128:(c + 1) * 128,
                                           r * 128:(r + 1) * 128], in_=stg),
                  r=[("wstg", c % 2)], w=[("wqscr", wname)], tag=f"{wname}st")


def prep_ffn_wgwu_tile(e, wname, dram_src, dram_dst, r):
    st_ap, st_k = dma_slot(e, 2)
    e.DMA("sync", lambda g, r=r, st_ap=st_ap, dram_src=dram_src: g.dma_start(
        out=st_ap, in_=dram_src[r * 128:(r + 1) * 128, :]),
        w=st_k, tag=f"{wname}ld{r}")
    _quant_and_store_windows(e, wname, dram_dst, r, st_ap, 0, 1024, st_k)


def prep_ffn_wd_tile(e, r):
    old_lo = e.scr_lo
    e.scr_lo = 3
    st_ap = e.scr[:, 0:1536]
    st_k = [("scr", 0), ("scr", 1), ("scr", 2)]
    for half in range(2):
        c0 = half * 1408
        e.DMA("sync", lambda g, r=r, c0=c0, st_ap=st_ap: g.dma_start(
            out=st_ap[:, 0:1408],
            in_=e.W["f_wd"][r * 128:(r + 1) * 128, c0:c0 + 1408]),
            w=st_k, tag=f"wdld{r}_{half}")
        _quant_and_store_windows(e, "wd", e.wdq, r, st_ap, c0, 1408, st_k)
    e.scr_lo = old_lo


def prep_plain_transposed(e, src_name, dst_base, dst_keyf, scale=None, tag="wT"):
    for r in range(8):
        st_ap, st_k = dma_slot(e, 2)
        e.DMA("sync", lambda g, r=r, st_ap=st_ap, src_name=src_name: g.dma_start(
            out=st_ap, in_=e.W[src_name][r * 128:(r + 1) * 128, :]),
            w=st_k, tag=f"{tag}ld{r}")
        tmp = e.wbuf[:, 8192:9216]
        tmp_k = wbk(8192, 9216)
        if scale is None:
            e.V(lambda v, st_ap=st_ap, tmp=tmp: v.tensor_copy(tmp, st_ap),
                r=st_k, w=tmp_k, tag=f"{tag}c{r}")
        else:
            e.V(lambda v, st_ap=st_ap, tmp=tmp, scale=scale: v.tensor_scalar(
                tmp, st_ap, scale, None, ALU.mult),
                r=st_k, w=tmp_k, tag=f"{tag}c{r}")
        dst3d = dst_base.rearrange("p (c t) -> p c t", c=8)[:, :, r * 128:(r + 1) * 128]
        transpose_1024(e, tmp, tmp_k, dst3d, dst_keyf(r), e.tpB, "tpB",
                       tag=f"{tag}T{r}")


def prep_attn_b_kv_weights(e):
    prep_plain_transposed(
        e, "b_Wk", e.wreg[:, 0:8192],
        lambda r: [("wreg", 0, c, r // 4) for c in range(8)],
        scale=SCL_B, tag="bWk")
    prep_plain_transposed(
        e, "b_Wv", e.wreg[:, 8192:16384],
        lambda r: [("wreg", 1, c, r // 4) for c in range(8)],
        tag="bWv")
    prep_direct_bias(e, "b_bv", 0, tag="dbbv")
    prep_direct_bias(e, "b_bo", 1, tag="dbbo")
    rf_ap, rf_k = dma_slot(e, 2)
    rowf = rf_ap[0:1, 0:1024]
    e.DMA("sync", lambda g, rowf=rowf: g.dma_start(
        out=rowf, in_=e.W["b_bq"].rearrange("(a b) -> a b", a=1)),
        w=rf_k, tag="rbq")
    rowb = e.lnf[0:1, 0:1024]
    e.V(lambda v, rowf=rowf, rowb=rowb: v.tensor_scalar(
        rowb, rowf, SCL_B, None, ALU.mult),
        r=rf_k, w=[("lnf", 0)], tag="rcbq")
    for c in range(8):
        e.T(lambda t, c=c, rowb=rowb: t.transpose(
            e.tpA[:, 2 * c:2 * c + 1], rowb[0:1, c * 128:(c + 1) * 128],
            e.misc[0:1, 0:1]),
            r=[("lnf", 0), ("misc", "id")], w=[("tpA",)], tag="bqbT")
    e.S(lambda s: s.copy(
        e.smf[:, 336:344].rearrange("p (c x) -> p c x", x=1),
        e.tpA[:, 0:16].rearrange("p (c x) -> p c x", x=2)[:, :, 0:1]),
        r=[("tpA",)], w=[("bqcol",)], tag="bqbTc")


def prep_attn_b_qo_weights(e):
    prep_plain_transposed(
        e, "b_Wq", e.wreg[:, 16384:24576],
        lambda r: [("wreg", 2, c, r // 4) for c in range(8)],
        scale=SCL_B, tag="bWq")
    prep_plain_transposed(
        e, "b_Wo", e.vprime[:, 0:8192],
        lambda r: [("vpk", (2 * c * 512 + r * 128) // 512) for c in range(8)],
        tag="bWo")



def stats_block(e, b):
    sq, sq_k = scr_slot(e, 2)
    e.S(lambda s, b=b, sq=sq: s.activation(
        sq, e.stream[:, b * 1024:(b + 1) * 1024], AF.Square),
        r=[("st", b)], w=sq_k, tag=f"stsq{b}")
    e.V(lambda v, b=b, sq=sq: v.tensor_reduce(
        e.sta[:, 96 + b:97 + b], sq, AX.X, ALU.add),
        r=sq_k, w=[("m2r", b)], tag=f"stm2{b}")
    e.V(lambda v, b=b: v.tensor_reduce(
        e.sta[:, 64 + b:65 + b], e.stream[:, b * 1024:(b + 1) * 1024],
        AX.X, ALU.add),
        r=[("st", b)], w=[("mur", b)], tag=f"stmu{b}")


def stats_agg(e):
    mu = e.sta[:, 64:96]
    m2 = e.sta[:, 96:128]
    rstd = e.sta[:, 0:32]
    nmr = e.sta[:, 32:64]
    mall = [("mur", b) for b in range(NBLK)] + [("m2r", b) for b in range(NBLK)]
    e.V(lambda v, mu=mu: v.tensor_scalar(mu, mu, 1.0 / 1024.0, None, ALU.mult),
        r=mall, w=[("muN",)], tag="stmun")
    e.V(lambda v, m2=m2: v.tensor_scalar(m2, m2, 1.0 / 1024.0, None, ALU.mult),
        r=mall, w=[("m2N",)], tag="stm2n")
    tmp = e.smf[:, 304:336]
    e.V(lambda v, mu=mu, tmp=tmp: v.tensor_tensor(tmp, mu, mu, ALU.mult),
        r=[("muN",)], w=[("mu2",)], tag="stmu2")
    e.V(lambda v, m2=m2, tmp=tmp: v.tensor_tensor(m2, m2, tmp, ALU.subtract),
        r=[("m2N",), ("mu2",)], w=[("var",)], tag="stvar")
    e.S(lambda s, rstd=rstd, m2=m2: s.activation(
        rstd, m2, AF.Ln, bias=EPS_OUT, scale=1.0),
        r=[("var",), ("cpk",)], w=[("lnv",)], tag="stln")
    e.S(lambda s, rstd=rstd: s.activation(rstd, rstd, AF.Exp, bias=0.0, scale=-0.5),
        r=[("lnv",), ("cpk",)], w=[("rstd",)], tag="stexp")
    e.V(lambda v, mu=mu, rstd=rstd, nmr=nmr: v.scalar_tensor_tensor(
        nmr, mu, -1.0, rstd, ALU.mult, ALU.mult),
        r=[("muN",), ("rstd",)], w=[("nmr",)], tag="stnmr")


def ln_apply(e, b, dst_ap, dst_keys, tag="ln"):
    return e.S(lambda s, b=b, dst_ap=dst_ap: s.activation(
        dst_ap, e.stream[:, b * 1024:(b + 1) * 1024], AF.Identity,
        bias=e.sta[:, 32 + b:33 + b], scale=e.sta[:, b:b + 1]),
        r=[("st", b), ("rstd",), ("nmr",)], w=list(dst_keys), tag=f"{tag}{b}")



def ofm_slot(e, g):
    if g < 2:
        return e.q2t[:, (g % 2) * 512:(g % 2) * 512 + 512], ("q2t", g % 2)
    if g < 4:
        return e.k2t[:, (g % 2) * 512:(g % 2) * 512 + 512], ("k2t", g % 2)
    return e.lnt[:, (g - 4) * 512:(g - 4) * 512 + 512], ("lnt", g - 4)


def attn_a(e):
    prot = [0]

    def nxt_pool():
        prot[0] = (prot[0] + 1) % 4
        return prot[0], e.pool[prot[0]], ("pool", prot[0])

    for seq in range(2):
        e.p.scope = "attnA_pA"
        e.V(lambda v: v.memset(e.acc[:], 0.0),
            w=[("accA",), ("accB",)], tag="accz")
        pend_vT = [None]
        for bt in range(BPS):
            gb = seq * BPS + bt
            buf = bt % 2
            lnt_ap = e.lnt[:, buf * 1024:(buf + 1) * 1024]
            if buf == 0:
                lnf_ap = e.lnf[:, 0:1024]
                lnf_keys = [("lnf", 0), ("lnf", 1)]
                lnf_rd = lambda kc: ("lnf", kc // 4)
            else:
                lnf_ap = e.sbd[:, 0:1024]
                lnf_keys = [("sbd", i) for i in range(8)]
                lnf_rd = lambda kc: ("sbd", kc)
            ln_apply(e, gb, lnt_ap, [("lnt", 2 * buf), ("lnt", 2 * buf + 1)], tag="alna")
            tp, tpn = (e.tpA, "tpA") if buf == 0 else (e.tpB, "tpB")
            transpose_1024(e, lnt_ap, [("lnt", 2 * buf), ("lnt", 2 * buf + 1)],
                           lnf_ap.rearrange("p (c x) -> p c x", x=128),
                           lnf_keys, tp, tpn, tag=f"alnT{gb}")
            for wslot, dstap, dstk in [(0, e.q2t, "q2t"), (1, e.k2t, "k2t")]:
                for h in range(2):
                    bi, bank, bkey = nxt_pool()
                    for kc in range(8):
                        e.T(lambda t, kc=kc, h=h, wslot=wslot, bank=bank,
                            lnf_ap=lnf_ap: t.matmul(
                                bank[:], lnf_ap[:, kc * 128:(kc + 1) * 128],
                                e.wreg[:, wslot * 8192 + kc * 1024 + h * 512:
                                       wslot * 8192 + kc * 1024 + (h + 1) * 512],
                                start=(kc == 0), stop=(kc == 7)),
                            r=[lnf_rd(kc), ("wreg", wslot, kc, h)], w=[bkey],
                            tag="aqk")
                    a_ap, a_k = scr_slot(e, 1)
                    m_ap, m_k = scr_slot(e, 1)
                    e.V(lambda v, bank=bank, a_ap=a_ap, h=h, wslot=wslot,
                        bkey=bkey: v.scalar_tensor_tensor(
                            a_ap, bank[:], 0.0,
                            e.btile[:, wslot * 1024 + h * 512:
                                    wslot * 1024 + (h + 1) * 512],
                            ALU.bypass, ALU.add),
                        r=[bkey, ("bt", wslot, h)], w=a_k, tag="abias")
                    e.V(lambda v, a_ap=a_ap, m_ap=m_ap: v.tensor_scalar(
                        m_ap, a_ap, 0.0, None, ALU.min),
                        r=a_k, w=m_k, tag="amin")
                    e.S(lambda s, m_ap=m_ap: s.activation(m_ap, m_ap, AF.Exp),
                        r=m_k + [("cpk",)], w=m_k, tag="aexp")
                    e.V(lambda v, a_ap=a_ap, m_ap=m_ap, dstap=dstap, h=h:
                        v.scalar_tensor_tensor(
                            dstap[:, h * 512:(h + 1) * 512], a_ap, 0.0, m_ap,
                            ALU.max, ALU.add),
                        r=a_k + m_k, w=[(dstk, h)], tag="aelu")
            if pend_vT[0] is not None:
                pend_vT[0]()
                pend_vT[0] = None
            prod, prod_k = scr_slot(e, 2)
            e.V(lambda v, prod=prod: v.tensor_tensor(prod, e.q2t[:], e.k2t[:],
                                                     ALU.mult),
                r=[("q2t", 0), ("q2t", 1), ("k2t", 0), ("k2t", 1)], w=prod_k,
                tag="aqk2")
            qks = e.smf[:, 0:16]
            rq = e.smf[:, 16:32]
            e.V(lambda v, prod=prod, qks=qks: v.tensor_reduce(
                qks, prod.rearrange("p (hh k) -> p hh k", k=64), AX.X, ALU.add),
                r=prod_k, w=[("qks",)], tag="aqks")
            e.V(lambda v, qks=qks: v.tensor_scalar(qks, qks, EPS_QK, None, ALU.add),
                r=[("qks",)], w=[("qks",)], tag="aqke")
            e.V(lambda v, qks=qks, rq=rq: v.reciprocal(rq, qks),
                r=[("qks",)], w=[("rq",)], tag="arq")
            for g in range(8):
                e.T(lambda t, g=g, bt=bt: t.matmul(
                    e.acc[:, g * 128:(g + 1) * 128],
                    e.k2t[:, g * 128:(g + 1) * 128],
                    e.q2t[:, g * 128:(g + 1) * 128],
                    start=False, stop=(bt == BPS - 1), skip_group_check=True),
                    r=[("q2t", g // 4), ("k2t", g // 4)],
                    w=[("accA",) if g < 4 else ("accB",)], tag="aS")
            vt, vt_k = scr_slot(e, 2)
            for h in range(2):
                bi, bank, bkey = nxt_pool()
                for kc in range(8):
                    e.T(lambda t, kc=kc, h=h, bank=bank, lnf_ap=lnf_ap: t.matmul(
                        bank[:], lnf_ap[:, kc * 128:(kc + 1) * 128],
                        e.wreg[:, 2 * 8192 + kc * 1024 + h * 512:
                               2 * 8192 + kc * 1024 + (h + 1) * 512],
                        start=(kc == 0), stop=(kc == 7)),
                        r=[lnf_rd(kc), ("wreg", 2, kc, h)], w=[bkey], tag="av2")
                e.V(lambda v, bank=bank, h=h, vt=vt, bkey=bkey:
                    v.scalar_tensor_tensor(
                        vt[:, h * 512:(h + 1) * 512], bank[:], 0.0,
                        e.btile[:, 2 * 1024 + h * 512:2 * 1024 + (h + 1) * 512],
                        ALU.bypass, ALU.add),
                    r=[bkey, ("bt", 2, h)], w=[vt_k[h]], tag="avb")
            e.V(lambda v, vt=vt, lnt_ap=lnt_ap: v.tensor_tensor(
                lnt_ap.rearrange("p (hh k) -> p hh k", k=64),
                vt.rearrange("p (hh k) -> p hh k", k=64),
                bcast_inner(e.smf[:, 16:32], 64), ALU.mult),
                r=vt_k + [("rq",)], w=[("lnt", 2 * buf), ("lnt", 2 * buf + 1)], tag="avdiv")
            tp2, tp2n = (e.tpB, "tpB") if buf == 0 else (e.tpA, "tpA")
            vdst = e.vprime[:].rearrange("p (c t) -> p c t", c=8)[
                :, :, bt * 128:(bt + 1) * 128]
            def _vt(lnt_ap=lnt_ap, buf=buf, vdst=vdst, tp2=tp2, tp2n=tp2n,
                    bt=bt, gb=gb):
                transpose_1024(e, lnt_ap, [("lnt", 2 * buf), ("lnt", 2 * buf + 1)], vdst,
                               [("vpk", 4 * c + bt // 4) for c in range(8)],
                               tp2, tp2n, tag=f"avT{gb}")
            pend_vT[0] = _vt
        if pend_vT[0] is not None:
            pend_vT[0]()
            pend_vT[0] = None
        e.p.scope = "attnA_ev"
        e.V(lambda v: v.memset(e.sbd[:], 0.0),
            w=[("sbd", g) for g in range(8)], tag="sbz")
        for g in range(8):
            e.S(lambda s, g=g: s.copy(
                e.sbd[0:64, g * 128:g * 128 + 64],
                e.acc[0:64, g * 128:g * 128 + 64]),
                r=[("accA",) if g < 4 else ("accB",), ("sbd", g)], w=[("sbd", g)], tag="sev0")
            e.S(lambda s, g=g: s.copy(
                e.sbd[64:128, g * 128 + 64:g * 128 + 128],
                e.acc[64:128, g * 128 + 64:g * 128 + 128]),
                r=[("accA",) if g < 4 else ("accB",), ("sbd", g)], w=[("sbd", g)], tag="sev1")
        e.p.scope = "attnA_pB"
        banks6 = [(e.pool[i][:], ("pool", i)) for i in range(4)]
        prot6 = [0]

        def nxt_pool6():
            prot6[0] = (prot6[0] + 1) % 4
            ap, key = banks6[prot6[0]]
            return prot6[0], ap, key
        for sp in range(NSPAN):
            m0 = sp * 512
            for g in range(8):
                bi, bank, bkey = nxt_pool6()
                e.T(lambda t, g=g, m0=m0, bank=bank: t.matmul(
                    bank, e.sbd[:, g * 128:(g + 1) * 128],
                    e.vprime[:, g * 2048 + m0:g * 2048 + m0 + 512],
                    start=True, stop=True),
                    r=[("sbd", g), ("vpk", 4 * g + sp)], w=[bkey], tag="aob")
                oap, okey = ofm_slot(e, g)
                e.S(lambda s, bank=bank, oap=oap, bkey=bkey: s.copy(oap, bank),
                    r=[bkey], w=[okey], tag="aoc")
            for mb in range(4):
                gb = seq * BPS + sp * 4 + mb
                y = e.scr[:, 1024:2048]
                y_k = [("scr", 2), ("scr", 3)]
                for h in range(2):
                    bi, bank, bkey = nxt_pool6()
                    for ec in range(8):
                        oap, okey = ofm_slot(e, ec)
                        e.T(lambda t, ec=ec, h=h, mb=mb, bank=bank, oap=oap:
                            t.matmul(
                                bank, oap[:, mb * 128:(mb + 1) * 128],
                                e.wbuf[:, ec * 1024 + h * 512:
                                       ec * 1024 + (h + 1) * 512],
                                start=(ec == 0), stop=(ec == 7)),
                            r=[okey] + wbk(ec * 1024, (ec + 1) * 1024),
                            w=[bkey], tag="awo")
                    e.V(lambda v, bank=bank, h=h, y=y, bkey=bkey:
                        v.scalar_tensor_tensor(
                            y[:, h * 512:(h + 1) * 512], bank, 0.0,
                            e.btile[:, 3 * 1024 + h * 512:3 * 1024 + (h + 1) * 512],
                            ALU.bypass, ALU.add),
                        r=[bkey, ("bt", 3, h)], w=[y_k[h]], tag="ayb")
                lnr = e.scr[:, 2048:3072]
                lnr_k = [("scr", 4), ("scr", 5)]
                ln_apply(e, gb, lnr, lnr_k, tag="alnr")
                e.V(lambda v, y=y, lnr=lnr: v.tensor_tensor(y, y, lnr, ALU.add),
                    r=y_k + lnr_k, w=y_k, tag="ay")
                sq = e.scr[:, 3072:4096]
                sq_k = [("scr", 6), ("scr", 7)]
                e.S(lambda s, y=y, sq=sq: s.activation(sq, y, AF.Square),
                    r=y_k, w=sq_k, tag="aisq")
                mu1 = e.smf[:, 32:33]
                m21 = e.smf[:, 33:34]
                t1 = e.smf[:, 34:35]
                r2d = e.smf[:, 35:36]
                nm2 = e.smf[:, 36:37]
                e.V(lambda v, y=y, mu1=mu1: v.tensor_reduce(mu1, y, AX.X, ALU.add),
                    r=y_k, w=[("imu",)], tag="aimu")
                e.V(lambda v, sq=sq, m21=m21: v.tensor_reduce(m21, sq, AX.X, ALU.add),
                    r=sq_k, w=[("im2",)], tag="aim2")
                e.V(lambda v, mu1=mu1: v.tensor_scalar(
                    mu1, mu1, 1.0 / 1024.0, None, ALU.mult),
                    r=[("imu",)], w=[("imu",)], tag="aimun")
                e.V(lambda v, m21=m21: v.tensor_scalar(
                    m21, m21, 1.0 / 1024.0, None, ALU.mult),
                    r=[("im2",)], w=[("im2",)], tag="aim2n")
                e.V(lambda v, mu1=mu1, t1=t1: v.tensor_tensor(t1, mu1, mu1, ALU.mult),
                    r=[("imu",)], w=[("imusq",)], tag="aimu2")
                e.V(lambda v, m21=m21, t1=t1: v.tensor_tensor(
                    m21, m21, t1, ALU.subtract),
                    r=[("im2",), ("imusq",)], w=[("ivar",)], tag="aivar")
                e.S(lambda s, r2d=r2d, m21=m21: s.activation(
                    r2d, m21, AF.Ln, bias=EPS_IN, scale=1.0),
                    r=[("ivar",), ("cpk",)], w=[("ilnv",)], tag="ailn")
                e.S(lambda s, r2d=r2d: s.activation(
                    r2d, r2d, AF.Exp, bias=0.0, scale=-0.5),
                    r=[("ilnv",), ("cpk",)], w=[("irstd",)], tag="aiex")
                e.V(lambda v, mu1=mu1, r2d=r2d, nm2=nm2: v.scalar_tensor_tensor(
                    nm2, mu1, -1.0, r2d, ALU.mult, ALU.mult),
                    r=[("imu",), ("irstd",)], w=[("inmr",)], tag="ainm")
                z = e.scr[:, 2048:3072]
                z_k = [("scr", 4), ("scr", 5)]
                e.S(lambda s, y=y, z=z, r2d=r2d, nm2=nm2: s.activation(
                    z, y, AF.Identity, bias=nm2, scale=r2d),
                    r=y_k + [("irstd",), ("inmr",)], w=z_k, tag="aiz")
                e.P(lambda v, z=z: v.tensor_tensor(
                    z, z, e.btile[:, 4 * 1024:5 * 1024], ALU.mult),
                    r=z_k + [("bt", 4, 0), ("bt", 4, 1)], w=z_k, tag="aig")
                e.P(lambda v, z=z: v.tensor_tensor(
                    z, z, e.btile[:, 5 * 1024:6 * 1024], ALU.add),
                    r=z_k + [("bt", 5, 0), ("bt", 5, 1)], w=z_k, tag="aib")
                e.P(lambda v, z=z, gb=gb: v.tensor_tensor(
                    e.stream[:, gb * 1024:(gb + 1) * 1024],
                    z, e.stream[:, gb * 1024:(gb + 1) * 1024], ALU.add),
                    r=z_k + [("st", gb)], w=[("st", gb)], tag="ares")
                stats_block(e, gb)



def ffn_xquant(e, mac):
    for tb in range(4):
        ffn_xquant_tb(e, mac, tb)


def ffn_xquant_tb(e, mac, tb):
    if True:
        gb = mac * 4 + tb
        lnq, lnq_k = scr_slot(e, 2)
        ln_apply(e, gb, lnq, lnq_k, tag="flna")
        xn, xn_k = scr_slot(e, 2)
        c1a, c1_k = scr_slot(e, 2)
        ta, t_k = scr_slot(e, 2)
        bufs = bf16_bufs(e, 1024)
        bufs["xn"] = (xn, xn_k)
        bufs["c1"] = (c1a, c1_k)
        bufs["t"] = (ta, t_k)
        dst_bf = e.lnt[:, 0:1024]
        quant_window(e, lnq, lnq_k, 1024, dst_bf, [("lnt", 0), ("lnt", 1)],
                     bufs, tag=f"fxq{gb}")
        xdst = e.vprime[:, 0:4096].rearrange(
            "p (c t) -> p c t", c=8)[:, :, tb * 128:(tb + 1) * 128]
        tp, tpn = (e.tpA, "tpA") if tb % 2 == 0 else (e.tpB, "tpB")
        transpose_1024(e, dst_bf, [("lnt", 0), ("lnt", 1)], xdst,
                       [("vpk", c) for c in range(8)], tp, tpn,
                       tag=f"fxT{gb}")


def ffn_alpha(e, mac, nxt_pool):
    pend = []
    for j, (n0, nw) in enumerate(NCHUNKS):
        jb = j % 2
        wg_ap = e.wbuf[:, jb * 4096:(jb + 1) * 4096]
        wu_ap = e.wreg[:, 2 * 8192 + jb * 4096:2 * 8192 + (jb + 1) * 4096]
        wg_keys = wbk(jb * 4096, (jb + 1) * 4096)
        wu_keys = wreg_keys(2, 4 * jb, 4 * jb + 4, 0, 2)
        e.DMA("sync", lambda g, n0=n0, nw=nw, wg_ap=wg_ap: g.dma_start(
            out=wg_ap.rearrange("p (kc n) -> p kc n", n=512)[:, :, 0:nw],
            in_=e.wgq.rearrange("(kc p) n -> p kc n", p=128)[:, :, n0:n0 + nw]),
            r=[("wqscr", "wg")], w=wg_keys, tag=f"fwg{mac}_{j}")
        e.DMA("sync", lambda g, n0=n0, nw=nw, wu_ap=wu_ap: g.dma_start(
            out=wu_ap.rearrange("p (kc n) -> p kc n", n=512)[:, :, 0:nw],
            in_=e.wuq.rearrange("(kc p) n -> p kc n", p=128)[:, :, n0:n0 + nw]),
            r=[("wqscr", "wu")], w=wu_keys, tag=f"fwu{mac}_{j}")
        for tb in range(4):
            biG, bankG, bkG = nxt_pool()
            biU, bankU, bkU = nxt_pool()
            for kc in range(8):
                e.T(lambda t, kc=kc, tb=tb, nw=nw, bankG=bankG, wg_ap=wg_ap:
                    t.matmul(
                        bankG[:, 0:nw],
                        e.vprime[:, kc * 512 + tb * 128:kc * 512 + (tb + 1) * 128],
                        wg_ap[:, kc * 512:kc * 512 + nw],
                        start=(kc == 0), stop=(kc == 7)),
                    r=[("vpk", kc)] + wg_keys, w=[bkG], tag="fg")
                e.T(lambda t, kc=kc, tb=tb, nw=nw, bankU=bankU, wu_ap=wu_ap:
                    t.matmul(
                        bankU[:, 0:nw],
                        e.vprime[:, kc * 512 + tb * 128:kc * 512 + (tb + 1) * 128],
                        wu_ap[:, kc * 512:kc * 512 + nw],
                        start=(kc == 0), stop=(kc == 7)),
                    r=[("vpk", kc)] + wu_keys, w=[bkU], tag="fu")
            if len(pend) >= 2:
                pend.pop(0)()
            sg, sg_k = scr_slot(e, 1)
            e.S(lambda s, bankG=bankG, sg=sg, nw=nw, bkG=bkG: s.activation(
                sg[:, 0:nw], bankG[:, 0:nw], AF.Silu),
                r=[bkG, ("cpk",)], w=sg_k, tag="fsilu")
            h_ap, h_k = sg, sg_k
            e.V(lambda v, bankU=bankU, sg=sg, nw=nw, bkU=bkU:
                v.tensor_tensor(
                    sg[:, 0:nw], sg[:, 0:nw], bankU[:, 0:nw], ALU.mult),
                r=sg_k + [bkU], w=h_k, tag="fh")
            xn, xn_k = scr_slot(e, 1)
            c1a, c1_k = scr_slot(e, 1)
            ta, t_k = scr_slot(e, 1)
            par = (j * 4 + tb) % 2
            bufs = bf16_bufs(e, nw, par)
            bufs["xn"] = (xn[:, 0:nw], xn_k)
            bufs["c1"] = (c1a[:, 0:nw], c1_k)
            bufs["t"] = (ta[:, 0:nw], t_k)
            dst_bf = e.lnt[:, par * 512:par * 512 + nw]
            dst_keys = [("lnt", par)]
            quant_window(e, h_ap[:, 0:nw], h_k, nw, dst_bf, dst_keys,
                         bufs, tag=f"fhq{mac}_{j}_{tb}")

            def _ht(j=j, tb=tb, nw=nw, dst_bf=dst_bf, par=par):
                nb = nw // 128
                tp, tpn = (e.tpA, "tpA") if (j % 2 == 0) else (e.tpB, "tpB")
                sl0 = (j * 4) % 8
                for ci in range(nb):
                    e.T(lambda t, ci=ci, tp=tp, sl0=sl0, dst_bf=dst_bf, par=par:
                        t.transpose(
                            tp[:, (sl0 + ci) * 128:(sl0 + ci + 1) * 128],
                            dst_bf[:, ci * 128:(ci + 1) * 128], e.misc[:, 0:128]),
                        r=[("lnt", par), ("misc", "id")], w=[(tpn,)], tag="fhT")
                vdst = e.vprime[:, 4096 + j * 4 * 512:
                                4096 + (j * 4 + nb) * 512].rearrange(
                    "p (c t) -> p c t", t=512)[:, :, tb * 128:(tb + 1) * 128]
                e.S(lambda s, tp=tp, sl0=sl0, nb=nb, vdst=vdst: s.copy(
                    vdst,
                    tp[:, sl0 * 128:(sl0 + nb) * 128].rearrange(
                        "p (c x) -> p c x", x=128)),
                    r=[(tpn,)],
                    w=[("vpk", 8 + j * 4 + ci) for ci in range(nb)], tag="fhTc")
            pend.append(_ht)
    while pend:
        pend.pop(0)()


def ffn_beta_mh(e, mac, mh, final, nxt_pool, hooks=None):
    banks = []
    for tb in range(4):
        bi, bank, bkey = nxt_pool()
        banks.append((bank, bkey))
    for Ic in range(22):
        sl = Ic % 4
        wd_ap = e.wbuf[:, 8192 + sl * 512:8192 + (sl + 1) * 512]
        wd_keys = wbk(8192 + sl * 512, 8192 + (sl + 1) * 512)
        e.DMA("gpsimd", lambda g, Ic=Ic, mh=mh, wd_ap=wd_ap: g.dma_start(
            out=wd_ap,
            in_=e.wdq[Ic * 128:(Ic + 1) * 128, mh * 512:(mh + 1) * 512]),
            r=[("wqscr", "wd")], w=wd_keys, tag=f"fwd{mac}_{mh}_{Ic}")
        for tb in range(4):
            bank, bkey = banks[tb]
            e.T(lambda t, Ic=Ic, tb=tb, bank=bank, wd_ap=wd_ap: t.matmul(
                bank[:, 0:512],
                e.vprime[:, 4096 + Ic * 512 + tb * 128:
                         4096 + Ic * 512 + (tb + 1) * 128],
                wd_ap, start=(Ic == 0), stop=(Ic == 21),
                skip_group_check=True),
                r=[("vpk", 8 + Ic)] + wd_keys, w=[bkey], tag="fd")
        if hooks and Ic in hooks:
            hooks[Ic]()
    for tb in range(4):
        gb = mac * 4 + tb
        bank, bkey = banks[tb]
        if not final:
            e.V(lambda v, bank=bank, gb=gb, mh=mh, bkey=bkey:
                v.tensor_tensor(
                    e.stream[:, gb * 1024 + mh * 512:gb * 1024 + (mh + 1) * 512],
                    bank[:, 0:512],
                    e.stream[:, gb * 1024 + mh * 512:gb * 1024 + (mh + 1) * 512],
                    ALU.add),
                r=[bkey, ("st", gb)], w=[("st", gb)], tag="fres")
            if mh == 1:
                stats_block(e, gb)
        else:
            o_ap, o_k = scr_slot(e, 1)
            e.V(lambda v, bank=bank, gb=gb, mh=mh, o_ap=o_ap, bkey=bkey:
                v.tensor_tensor(
                    o_ap, bank[:, 0:512],
                    e.stream[:, gb * 1024 + mh * 512:gb * 1024 + (mh + 1) * 512],
                    ALU.add),
                r=[bkey, ("st", gb)], w=o_k, tag="fout")
            e.DMA("gpsimd", lambda g, gb=gb, mh=mh, o_ap=o_ap: g.dma_start(
                out=e.out[gb * 128:(gb + 1) * 128, mh * 512:(mh + 1) * 512],
                in_=o_ap),
                r=o_k, w=[("outd", gb, mh)], tag="fst")


def ffn(e, final):
    old_lo = e.scr_lo
    e.scr_lo = 0
    prot = [0]
    banks4 = [(e.pool[i][:], ("pool", i)) for i in range(4)]

    def nxt_pool():
        prot[0] = (prot[0] + 1) % 4
        ap, key = banks4[prot[0]]
        return prot[0], ap, key

    base = e.p.scope
    e.p.scope = base + "_xq"
    ffn_xquant(e, 0)
    for mac in range(8):
        e.p.scope = base + "_al"
        ffn_alpha(e, mac, nxt_pool)
        e.p.scope = base + "_be"
        if mac + 1 < 8:
            h0 = {6: (lambda mac=mac: ffn_xquant_tb(e, mac + 1, 0)),
                  12: (lambda mac=mac: ffn_xquant_tb(e, mac + 1, 1)),
                  18: (lambda mac=mac: ffn_xquant_tb(e, mac + 1, 2))}
            h1 = {6: (lambda mac=mac: ffn_xquant_tb(e, mac + 1, 3))}
        else:
            h0 = h1 = None
        ffn_beta_mh(e, mac, 0, final, nxt_pool, hooks=h0)
        ffn_beta_mh(e, mac, 1, final, nxt_pool, hooks=h1)
    e.scr_lo = old_lo



def attn_b(e):
    prot = [0]

    def nxt_pool():
        prot[0] = (prot[0] + 1) % 4
        return prot[0], e.pool[prot[0]], ("pool", prot[0])

    for seq in range(2):
        e.p.scope = "attnB_pA"
        e.V(lambda v: v.memset(e.acc[:], 0.0),
            w=[("accA",), ("accB",)], tag="bacz")
        for bt in range(BPS):
            gb = seq * BPS + bt
            buf = bt % 2
            lnt_ap = e.lnt[:, buf * 1024:(buf + 1) * 1024]
            if buf == 0:
                lnf_ap = e.lnf[:, 0:1024]
                lnf_keys = [("lnf", 0), ("lnf", 1)]
                lnf_rd = lambda kc: ("lnf", kc // 4)
            else:
                lnf_ap = e.sbd[:, 0:1024]
                lnf_keys = [("sbd", i) for i in range(8)]
                lnf_rd = lambda kc: ("sbd", kc)
            ln_apply(e, gb, lnt_ap, [("lnt", 2 * buf), ("lnt", 2 * buf + 1)], tag="blna")
            tp, tpn = (e.tpA, "tpA") if buf == 0 else (e.tpB, "tpB")
            transpose_1024(e, lnt_ap, [("lnt", 2 * buf), ("lnt", 2 * buf + 1)],
                           lnf_ap.rearrange("p (c x) -> p c x", x=128),
                           lnf_keys, tp, tpn, tag=f"blnT{gb}")
            for h in range(2):
                bi, bank, bkey = nxt_pool()
                for kc in range(8):
                    e.T(lambda t, kc=kc, h=h, bank=bank, lnf_ap=lnf_ap: t.matmul(
                        bank[:], lnf_ap[:, kc * 128:(kc + 1) * 128],
                        e.wreg[:, 0 * 8192 + kc * 1024 + h * 512:
                               0 * 8192 + kc * 1024 + (h + 1) * 512],
                        start=(kc == 0), stop=(kc == 7)),
                        r=[lnf_rd(kc), ("wreg", 0, kc, h)], w=[bkey], tag="bk")
                sk, sk_k = scr_slot(e, 1)
                e.S(lambda s, bank=bank, sk=sk, bkey=bkey: s.activation(
                    sk, bank[:], AF.Silu),
                    r=[bkey, ("cpk",)], w=sk_k, tag="bksil")
                e.V(lambda v, sk=sk, h=h: v.tensor_scalar(
                    e.q2t[:, h * 512:(h + 1) * 512], sk, 1.0, None, ALU.add),
                    r=sk_k, w=[("q2t", h)], tag="bkphi")
            for h in range(2):
                bi, bank, bkey = nxt_pool()
                for kc in range(8):
                    e.T(lambda t, kc=kc, h=h, bank=bank, lnf_ap=lnf_ap: t.matmul(
                        bank[:], lnf_ap[:, kc * 128:(kc + 1) * 128],
                        e.wreg[:, 1 * 8192 + kc * 1024 + h * 512:
                               1 * 8192 + kc * 1024 + (h + 1) * 512],
                        start=(kc == 0), stop=(kc == 7)),
                        r=[lnf_rd(kc), ("wreg", 1, kc, h)], w=[bkey], tag="bv")
                e.V(lambda v, bank=bank, h=h, bkey=bkey: v.scalar_tensor_tensor(
                    e.k2t[:, h * 512:(h + 1) * 512], bank[:], 0.0,
                    e.btile[:, 0 * 1024 + h * 512:0 * 1024 + (h + 1) * 512],
                    ALU.bypass, ALU.add),
                    r=[bkey, ("bt", 0, h)], w=[("k2t", h)], tag="bvb")
            for g in range(8):
                e.T(lambda t, g=g, bt=bt: t.matmul(
                    e.acc[:, g * 128:(g + 1) * 128],
                    e.q2t[:, g * 128:(g + 1) * 128],
                    e.k2t[:, g * 128:(g + 1) * 128],
                    start=False, stop=(bt == BPS - 1), skip_group_check=True),
                    r=[("q2t", g // 4), ("k2t", g // 4)],
                    w=[("accA",) if g < 4 else ("accB",)], tag="bkv")
        e.V(lambda v: v.memset(e.sbd[:], 0.0),
            w=[("sbd", g) for g in range(8)], tag="bsbz")
        for g in range(8):
            e.S(lambda s, g=g: s.copy(
                e.sbd[0:64, g * 128:g * 128 + 64],
                e.acc[0:64, g * 128:g * 128 + 64]),
                r=[("accA",) if g < 4 else ("accB",), ("sbd", g)], w=[("sbd", g)], tag="bev0")
            e.S(lambda s, g=g: s.copy(
                e.sbd[64:128, g * 128 + 64:g * 128 + 128],
                e.acc[64:128, g * 128 + 64:g * 128 + 128]),
                r=[("accA",) if g < 4 else ("accB",), ("sbd", g)], w=[("sbd", g)], tag="bev1")
        e.p.scope = "attnB_pB"
        for sp in range(NSPAN):
            for mb in range(4):
                gb = seq * BPS + sp * 4 + mb
                buf = mb % 2
                lnt_ap = e.lnt[:, buf * 1024:(buf + 1) * 1024]
                ln_apply(e, gb, lnt_ap, [("lnt", 2 * buf), ("lnt", 2 * buf + 1)], tag="blnb")
                tp, tpn = (e.tpA, "tpA") if buf == 0 else (e.tpB, "tpB")
                dst3d = e.wbuf[:, 0:4096].rearrange(
                    "p (c t) -> p c t", c=8)[:, :, mb * 128:(mb + 1) * 128]
                transpose_1024(e, lnt_ap, [("lnt", 2 * buf), ("lnt", 2 * buf + 1)], dst3d,
                               wbk(0, 4096), tp, tpn, tag=f"bl3T{gb}")
            for ic in range(8):
                bi, bank, bkey = nxt_pool()
                for jc in range(8):
                    e.T(lambda t, ic=ic, jc=jc, bank=bank: t.matmul(
                        bank[:],
                        e.wreg[:, 2 * 8192 + jc * 1024 + ic * 128:
                               2 * 8192 + jc * 1024 + (ic + 1) * 128],
                        e.wbuf[:, jc * 512:(jc + 1) * 512],
                        start=(jc == 0), stop=(jc == 7)),
                        r=wbk(jc * 512, (jc + 1) * 512) +
                        [("wreg", 2, jc, ic // 4)],
                        w=[bkey], tag="bq")
                sq, sq_k = scr_slot(e, 1)
                e.S(lambda s, bank=bank, sq=sq, ic=ic, bkey=bkey: s.activation(
                    sq, bank[:], AF.Silu, bias=e.smf[:, 336 + ic:337 + ic],
                    scale=1.0),
                    r=[bkey, ("bqcol",), ("cpk",)], w=sq_k, tag="bqsil")
                e.V(lambda v, sq=sq, ic=ic: v.tensor_scalar(
                    e.vprime[:, 8192 + ic * 512:8192 + (ic + 1) * 512],
                    sq, 1.0, None, ALU.add),
                    r=sq_k, w=[("vpk", 16 + ic)], tag="bqphi")
            for g in range(8):
                bi, bank, bkey = nxt_pool()
                e.T(lambda t, g=g, bank=bank: t.matmul(
                    bank[:], e.sbd[:, g * 128:(g + 1) * 128],
                    e.vprime[:, 8192 + g * 512:8192 + (g + 1) * 512],
                    start=True, stop=True),
                    r=[("sbd", g), ("vpk", 16 + g)], w=[bkey], tag="bo")
                e.S(lambda s, g=g, bank=bank, bkey=bkey: s.copy(
                    e.vprime[:, 12288 + g * 512:12288 + (g + 1) * 512], bank[:]),
                    r=[bkey], w=[("vpk", 24 + g)], tag="boc")
            for mb in range(4):
                gb = seq * BPS + sp * 4 + mb
                for h in range(2):
                    bi, bank, bkey = nxt_pool()
                    for ec in range(8):
                        e.T(lambda t, ec=ec, h=h, mb=mb, bank=bank: t.matmul(
                            bank[:],
                            e.vprime[:, 12288 + ec * 512 + mb * 128:
                                     12288 + ec * 512 + (mb + 1) * 128],
                            e.vprime[:, ec * 1024 + h * 512:
                                     ec * 1024 + (h + 1) * 512],
                            start=(ec == 0), stop=(ec == 7)),
                            r=[("vpk", 24 + ec)] +
                            vpk(ec * 1024 + h * 512, ec * 1024 + (h + 1) * 512),
                            w=[bkey], tag="bwo")
                    o_ap, o_k = scr_slot(e, 1)
                    e.V(lambda v, bank=bank, h=h, o_ap=o_ap, bkey=bkey:
                        v.scalar_tensor_tensor(
                            o_ap, bank[:], 0.0,
                            e.btile[:, 1 * 1024 + h * 512:1 * 1024 + (h + 1) * 512],
                            ALU.bypass, ALU.add),
                        r=[bkey, ("bt", 1, h)], w=o_k, tag="bob")
                    e.P(lambda v, o_ap=o_ap, gb=gb, h=h: v.tensor_tensor(
                        e.stream[:, gb * 1024 + h * 512:gb * 1024 + (h + 1) * 512],
                        o_ap,
                        e.stream[:, gb * 1024 + h * 512:gb * 1024 + (h + 1) * 512],
                        ALU.add),
                        r=o_k + [("st", gb)], w=[("st", gb)], tag="bres")
                    if h == 1:
                        stats_block(e, gb)



_WNAMES = [
    "L0_attn_a_Wq", "L0_attn_a_bq", "L0_attn_a_Wk", "L0_attn_a_bk",
    "L0_attn_a_Wv", "L0_attn_a_bv", "L0_attn_a_Wq2", "L0_attn_a_bq2",
    "L0_attn_a_Wk2", "L0_attn_a_bk2", "L0_attn_a_Wv2", "L0_attn_a_bv2",
    "L0_attn_a_Wo", "L0_attn_a_bo", "L0_attn_a_gamma", "L0_attn_a_beta",
    "L0_attn_b_Wq", "L0_attn_b_bq", "L0_attn_b_Wk", "L0_attn_b_Wv",
    "L0_attn_b_bv", "L0_attn_b_Wo", "L0_attn_b_bo",
    "L0_ffn_w_gate", "L0_ffn_w_up", "L0_ffn_w_down",
]


def kernel(**inputs):
    x = np.asarray(inputs["L0_attn_a_x"], dtype=np.float32)
    nc = build_program()
    in_maps = []
    for c in range(8):
        m = {"xin": np.ascontiguousarray(x[2 * c:2 * c + 2].reshape(NBLK * 128, D))}
        for n in _WNAMES:
            m[n] = np.ascontiguousarray(inputs[n], dtype=np.float32)
        in_maps.append(m)
    res = run_bass_kernel_spmd(nc, in_maps, core_ids=list(range(8)))
    out = np.concatenate(
        [res.results[c]["out"].reshape(2, 2048, D) for c in range(8)], axis=0)
    return out


if __name__ == "__main__":
    import os
    os.environ.setdefault("KDEBUG", "1")
    nc = build_program()
    print("build ok")


# version 20
# speedup vs baseline: 1.1898x; 1.1898x over previous
import numpy as np
import ml_dtypes
import concourse.bass as bass
import concourse.mybir as mybir
from concourse.bass_utils import run_bass_kernel_spmd

F32 = mybir.dt.float32
BF16 = mybir.dt.bfloat16
ALU = mybir.AluOpType
AF = mybir.ActivationFunctionType
AX = mybir.AxisListType
BFNP = ml_dtypes.bfloat16

D = 1024
IFFN = 2816
NBLK = 32
BPS = 16
NSPAN = 4
C05 = float(3 * 2**22) * 0.5
C1F = float(3 * 2**22)
C2F = float(3 * 2**23)
SCL_B = float(64 ** -0.25)
EPS_OUT = 1e-5
EPS_IN = 1e-3
EPS_QK = 1e-6
NCHUNKS = [(j * 512, min(512, IFFN - j * 512)) for j in range((IFFN + 511) // 512)]

ENGS = ("sync", "gpsimd", "vector", "scalar", "tensor")
DMAE = {"sync", "gpsimd"}


def bcast_inner(ap, n):
    return bass.AP(ap.tensor, ap.offset, [list(x) for x in ap.ap] + [[0, n]])


class Prog:
    def __init__(self, nc):
        self.nc = nc
        self.ops = {e: [] for e in ENGS}
        self.reg = {}
        self.clock = {e: {e2: -1 for e2 in ENGS} for e in ENGS}
        self.scope = "init"

    def emit(self, eng, builder, reads=(), writes=(), dma=False, tag=""):
        idx = len(self.ops[eng])
        waits = {}

        def add_wait(e2, i2):
            if self.clock[eng][e2] >= i2:
                return
            if waits.get(e2, -1) < i2:
                waits[e2] = i2

        for k in reads:
            r = self.reg.get(k)
            if r and r["w"] is not None:
                add_wait(*r["w"])
        for k in writes:
            r = self.reg.get(k)
            if r:
                if r["w"] is not None and r["w"][0] != eng:
                    add_wait(*r["w"])
                for e2, i2 in r["r"].items():
                    if e2 != eng:
                        add_wait(e2, i2)
        for e2, i2 in waits.items():
            self.ops[e2][i2]["inc"] = True
            if self.clock[eng][e2] < i2:
                self.clock[eng][e2] = i2
        self.ops[eng].append({"b": builder, "waits": waits, "inc": dma,
                              "dma": dma, "tag": tag, "sc": self.scope})
        for k in reads:
            r = self.reg.setdefault(k, {"w": None, "r": {}})
            r["r"][eng] = idx
        for k in writes:
            self.reg[k] = {"w": (eng, idx), "r": {}}
        return (eng, idx)

    def barrier_wait(self, eng, handles):
        waits = {}
        for (e2, i2) in handles:
            self.ops[e2][i2]["inc"] = True
            if waits.get(e2, -1) < i2:
                waits[e2] = i2
        self.ops[eng].append({"b": None, "waits": waits, "inc": False,
                              "dma": False, "tag": "barrier", "sc": self.scope})

    def finalize(self, sems):
        runend = {}
        for e in DMAE:
            ops = self.ops[e]
            n = len(ops)
            arr = [0] * n
            end = n - 1
            for i in range(n - 1, -1, -1):
                op = ops[i]
                if not op["dma"]:
                    end = i - 1 if i > 0 else -1
                    arr[i] = i
                    continue
                arr[i] = end if end >= i else i
                if op["waits"] or op["b"] is None:
                    end = i - 1
            runend[e] = arr
        for e in ENGS:
            for op in self.ops[e]:
                nw = {}
                for e2, i2 in op["waits"].items():
                    if e2 in DMAE and self.ops[e2][i2]["dma"]:
                        j = runend[e2][i2]
                        if j > i2:
                            self.ops[e2][j]["inc"] = True
                            i2 = j
                    nw[e2] = max(nw.get(e2, -1), i2)
                op["waits"] = nw
        prefix = {}
        for e in ENGS:
            c = 0
            arr = []
            for op in self.ops[e]:
                if op["inc"]:
                    c += 16 if op["dma"] else 1
                arr.append(c)
            prefix[e] = arr
        stats = {e: (len(self.ops[e]), prefix[e][-1] if self.ops[e] else 0)
                 for e in ENGS}

        def mk_body(ename):
            def body(eng):
                sem_own = sems[ename]
                cur = [None, None]
                for op in self.ops[ename]:
                    if op["sc"] != cur[0]:
                        if cur[0] is not None:
                            self.nc.leave_named_scope(cur[0], cur[1], False)
                        sid, _ = self.nc.enter_named_scope(op["sc"], False)
                        cur[0], cur[1] = op["sc"], sid
                    for e2 in sorted(op["waits"]):
                        i2 = op["waits"][e2]
                        eng.wait_ge(sems[e2], prefix[e2][i2])
                    if op["b"] is None:
                        continue
                    ins = op["b"](eng)
                    if op["inc"]:
                        ins.then_inc(sem_own, 16 if op["dma"] else 1)
                if cur[0] is not None:
                    self.nc.leave_named_scope(cur[0], cur[1], False)
            return body
        return mk_body, stats


class Env:
    def __init__(self, p, d):
        self.p = p
        for k, v in d.items():
            setattr(self, k, v)
        self.scr_rot = 2
        self.scr_lo = 2
        self.q_rot = 0
        self.qpend = []

    def V(self, builder, r=(), w=(), tag=""):
        return self.p.emit("vector", builder, r, w, tag=tag)

    def S(self, builder, r=(), w=(), tag=""):
        return self.p.emit("scalar", builder, r, w, tag=tag)

    def T(self, builder, r=(), w=(), tag=""):
        return self.p.emit("tensor", builder, r, w, tag=tag)

    def DMA(self, q, builder, r=(), w=(), tag=""):
        return self.p.emit(q, builder, r, w, dma=True, tag=tag)

    def P(self, builder, r=(), w=(), tag=""):
        return self.p.emit("gpsimd", builder, r, w, tag=tag)


def wreg_keys(slot, c0, c1, h0, h1):
    return [("wreg", slot, c, h) for c in range(c0, c1) for h in range(h0, h1)]


def vpk(col0, col1):
    return [("vpk", b) for b in range(col0 // 512, (col1 + 511) // 512)]


def wbk(col0, col1):
    return [("wbk", b) for b in range(col0 // 512, (col1 + 511) // 512)]


def qpush(e, bk):
    e.qpend.append(bk)
    if len(e.qpend) > 1:
        e.qpend.pop(0)()


def qflush(e):
    while e.qpend:
        e.qpend.pop(0)()


def scr_slot(e, n=1):
    hi = 8
    if e.scr_rot < e.scr_lo or e.scr_rot + n > hi:
        e.scr_rot = e.scr_lo
    s = e.scr_rot
    e.scr_rot += n
    return e.scr[:, s * 512:(s + n) * 512], [("scr", s + i) for i in range(n)]


def dma_slot(e, n=2):
    return e.scr[:, 0:n * 512], [("scr", i) for i in range(n)]


def qsc(e, nb):
    s = e.q_rot
    e.q_rot ^= 1
    base = 48 + s * 128
    return (e.smf[:, base:base + nb], ("scs", s),
            e.smf[:, base + 64:base + 64 + nb], ("rss", s))



def build_program():
    nc = bass.Bass("TRN2", target_bir_lowering=False, debug=False)
    p = Prog(nc)

    xin = nc.dram_tensor("xin", [NBLK * 128, D], F32, kind="ExternalInput").ap()
    out = nc.dram_tensor("out", [NBLK * 128, D], F32, kind="ExternalOutput").ap()
    W = {}
    for n in ["Wq", "Wk", "Wv", "Wq2", "Wk2", "Wv2", "Wo"]:
        W["a_" + n] = nc.dram_tensor("L0_attn_a_" + n, [D, D], F32, kind="ExternalInput").ap()
    for n in ["bq", "bk", "bv", "bq2", "bk2", "bv2", "bo", "gamma", "beta"]:
        W["a_" + n] = nc.dram_tensor("L0_attn_a_" + n, [D], F32, kind="ExternalInput").ap()
    for n in ["Wq", "Wk", "Wv", "Wo"]:
        W["b_" + n] = nc.dram_tensor("L0_attn_b_" + n, [D, D], F32, kind="ExternalInput").ap()
    for n in ["bq", "bv", "bo"]:
        W["b_" + n] = nc.dram_tensor("L0_attn_b_" + n, [D], F32, kind="ExternalInput").ap()
    W["f_wg"] = nc.dram_tensor("L0_ffn_w_gate", [IFFN, D], F32, kind="ExternalInput").ap()
    W["f_wu"] = nc.dram_tensor("L0_ffn_w_up", [IFFN, D], F32, kind="ExternalInput").ap()
    W["f_wd"] = nc.dram_tensor("L0_ffn_w_down", [D, IFFN], F32, kind="ExternalInput").ap()

    wgq = nc.dram_tensor("wgq_scr", [D, IFFN], BF16, kind="Internal").ap()
    wuq = nc.dram_tensor("wuq_scr", [D, IFFN], BF16, kind="Internal").ap()
    wdq = nc.dram_tensor("wdq_scr", [IFFN, D], BF16, kind="Internal").ap()

    ident_d = nc.inline_tensor(np.eye(128, dtype=BFNP), name="ident_bf")
    ones_d = nc.inline_tensor(np.ones((1, 128), dtype=BFNP), name="ones_bf")
    cvals = np.zeros((128, 16), dtype=np.float32)
    cvals[:, 1] = 1.0
    cvals[:, 2] = EPS_OUT
    cvals[:, 3] = EPS_IN
    cvals[:, 4] = EPS_QK
    cvals[:, 5] = C1F
    cvals[:, 6] = -C1F
    cvals[:, 7] = C2F
    cvals[:, 8] = -C2F
    cpk_d = nc.inline_tensor(cvals, name="cpack")

    import contextlib
    es = contextlib.ExitStack()
    sb = lambda name, shape, dt: es.enter_context(nc.sbuf_tensor(name, shape, dt))
    psA = lambda name, shape, dt: es.enter_context(nc.psum_tensor(name, shape, dt))
    with es:
        stream = sb("stream", [128, NBLK * 1024], BF16)
        wreg = sb("wreg", [128, 3 * 8192], BF16)
        vprime = sb("vprime", [128, 16384], BF16)
        wbuf = sb("wbuf", [128, 10240], BF16)
        scr = sb("scr", [128, 4096], F32)
        lnt = sb("lnt", [128, 2048], BF16)
        lnf = sb("lnf", [128, 1024], BF16)
        q2t = sb("q2t", [128, 1024], BF16)
        k2t = sb("k2t", [128, 1024], BF16)
        sbd = sb("sbd", [128, 1024], BF16)
        btile = sb("btile", [128, 6144], BF16)
        sta = sb("statst", [128, 128], F32)
        smf = sb("smf", [128, 384], F32)
        misc = sb("miscb", [128, 512], BF16)
        cpk = sb("cpk", [128, 16], F32)

        pool = [psA(f"pool{i}", [128, 512], F32) for i in range(4)]
        acc = psA("acc", [128, 1024], F32)
        tpA = psA("tpA", [128, 1024], BF16)
        tpB = psA("tpB", [128, 1024], BF16)

        sems = {}
        for en in ENGS:
            sems[en] = es.enter_context(nc.semaphore("sem_" + en))

        nc.const_aps.register(F32, 0.0, cpk[:, 0:1])
        nc.const_aps.register(F32, 1.0, cpk[:, 1:2])
        nc.const_aps.register(F32, EPS_OUT, cpk[:, 2:3])
        nc.const_aps.register(F32, EPS_IN, cpk[:, 3:4])
        nc.const_aps.register(F32, EPS_QK, cpk[:, 4:5])
        nc.const_aps.register(F32, C1F, cpk[:, 5:6])
        nc.const_aps.register(F32, -C1F, cpk[:, 6:7])
        nc.const_aps.register(F32, C2F, cpk[:, 7:8])
        nc.const_aps.register(F32, -C2F, cpk[:, 8:9])

        env = dict(p=p, nc=nc, xin=xin, out=out, W=W, wgq=wgq, wuq=wuq,
                   wdq=wdq, ident_d=ident_d, ones_d=ones_d, cpk_d=cpk_d,
                   stream=stream, wreg=wreg, vprime=vprime, wbuf=wbuf,
                   scr=scr, lnt=lnt, lnf=lnf, q2t=q2t, k2t=k2t, sbd=sbd,
                   btile=btile, sta=sta, smf=smf, misc=misc, cpk=cpk,
                   pool=pool, acc=acc, tpA=tpA, tpB=tpB)
        e = Env(p, env)
        build_body(e)
        mk_body, stats = p.finalize(sems)

        block = es.enter_context(nc.Block())
        block.sync(mk_body("sync"))
        block.gpsimd(mk_body("gpsimd"))
        block.vector(mk_body("vector"))
        block.scalar(mk_body("scalar"))
        block.tensor(mk_body("tensor"))

    import os
    if os.environ.get("KDEBUG"):
        print("program op counts (n_ops, n_incs):", stats)
    return nc


def build_body(e):
    p = e.p
    h_id = e.DMA("sync", lambda g: g.dma_start(out=e.misc[:, 0:128], in_=e.ident_d.ap()),
                 w=[("misc", "id")])
    h_on = e.DMA("sync", lambda g: g.dma_start(out=e.misc[0:1, 128:256], in_=e.ones_d.ap()),
                 w=[("misc", "ones")])
    h_cp = e.DMA("sync", lambda g: g.dma_start(out=e.cpk[:], in_=e.cpk_d.ap()),
                 w=[("cpk",)])
    p.barrier_wait("scalar", [h_id, h_on, h_cp])
    p.barrier_wait("vector", [h_id, h_on, h_cp])
    p.barrier_wait("tensor", [h_id, h_on, h_cp])

    import os
    ph = int(os.environ.get("KPHASES", "7"))
    p.scope = "loadx"
    load_x(e)
    if ph >= 2:
        p.scope = "prepA"
        prep_attn_a_weights(e)
    if ph >= 3:
        p.scope = "prepF"
        prep_ffn_weights(e)
    if ph >= 4:
        p.scope = "stats1"
        stats_agg(e)
        p.scope = "attnA"
        attn_a(e)
    if ph >= 5:
        p.scope = "stats2"
        stats_agg(e)
        p.scope = "prepBkv"
        prep_attn_b_kv_weights(e)
        p.scope = "ffn1"
        ffn(e, final=False)
    if ph >= 6:
        p.scope = "stats3"
        stats_agg(e)
        p.scope = "prepBqo"
        prep_attn_b_qo_weights(e)
        p.scope = "attnB"
        attn_b(e)
    if ph >= 7:
        p.scope = "stats4"
        stats_agg(e)
        p.scope = "ffn2"
        ffn(e, final=True)

    if p.ops["gpsimd"]:
        last = len(p.ops["gpsimd"]) - 1
        p.barrier_wait("gpsimd", [("gpsimd", last)])
    if p.ops["sync"]:
        p.barrier_wait("sync", [("sync", len(p.ops["sync"]) - 1)])


def load_x(e):
    for b in range(NBLK):
        st_ap, st_k = dma_slot(e, 2)
        e.DMA("sync", lambda g, b=b, st_ap=st_ap: g.dma_start(
            out=st_ap, in_=e.xin[b * 128:(b + 1) * 128, :]),
            w=st_k, tag=f"ldx{b}")
        e.V(lambda v, b=b, st_ap=st_ap: v.tensor_copy(
            e.stream[:, b * 1024:(b + 1) * 1024], st_ap),
            r=st_k, w=[("st", b)], tag=f"castx{b}")
        stats_block(e, b)



def transpose_1024(e, src_ap, src_keys, dst3d_ap, dst_keys, tp, tpn, tag="tp",
                   nblocks=8):
    for c in range(nblocks):
        e.T(lambda t, c=c, tp=tp, src_ap=src_ap: t.transpose(
            tp[:, c * 128:(c + 1) * 128], src_ap[:, c * 128:(c + 1) * 128],
            e.misc[:, 0:128]),
            r=list(src_keys) + [("misc", "id")], w=[(tpn,)], tag=f"{tag}t{c}")
    e.S(lambda s, tp=tp, dst3d_ap=dst3d_ap, nblocks=nblocks: s.copy(
        dst3d_ap, tp[:, 0:nblocks * 128].rearrange("p (c x) -> p c x", x=128)),
        r=[(tpn,)], w=list(dst_keys), tag=f"{tag}cp")


def bias_tile_from_row(e, rowbf_ap, row_keys, bslot, tag="bc"):
    for h in range(2):
        e.T(lambda t, h=h, rowbf_ap=rowbf_ap: t.matmul(
            e.pool[3][:, 0:512], e.misc[0:1, 128:256],
            rowbf_ap[0:1, h * 512:(h + 1) * 512], start=True, stop=True),
            r=list(row_keys) + [("misc", "ones")], w=[("pool", 3)], tag=f"{tag}mm{h}")
        e.S(lambda s, h=h, bslot=bslot: s.copy(
            e.btile[:, bslot * 1024 + h * 512:bslot * 1024 + (h + 1) * 512],
            e.pool[3][:, 0:512]),
            r=[("pool", 3)], w=[("bt", bslot, h)], tag=f"{tag}cp{h}")


def prep_direct_bias(e, vec_name, bslot, scale=None, tag="db"):
    rf_ap, rf_k = dma_slot(e, 2)
    rowf = rf_ap[0:1, 0:1024]
    e.DMA("sync", lambda g, rowf=rowf, vec_name=vec_name: g.dma_start(
        out=rowf, in_=e.W[vec_name].rearrange("(a b) -> a b", a=1)),
        w=rf_k, tag=f"{tag}ld")
    rowb = e.wbuf[0:1, 9216:10240]
    rowb_k = wbk(9216, 10240)
    if scale is None:
        e.V(lambda v, rowf=rowf, rowb=rowb: v.tensor_copy(rowb, rowf),
            r=rf_k, w=rowb_k, tag=f"{tag}c")
    else:
        e.V(lambda v, rowf=rowf, rowb=rowb, scale=scale: v.tensor_scalar(
            rowb, rowf, scale, None, ALU.mult),
            r=rf_k, w=rowb_k, tag=f"{tag}c")
    bias_tile_from_row(e, rowb, rowb_k, bslot, tag=tag)


def prep_attn_a_weights(e):
    for r in range(8):
        st_ap, st_k = dma_slot(e, 2)
        e.DMA("sync", lambda g, r=r, st_ap=st_ap: g.dma_start(
            out=st_ap, in_=e.W["a_Wo"][r * 128:(r + 1) * 128, :]),
            w=st_k, tag=f"wold{r}")
        e.V(lambda v, r=r, st_ap=st_ap: v.tensor_copy(
            e.wbuf[:, r * 1024:(r + 1) * 1024], st_ap),
            r=st_k, w=wbk(r * 1024, (r + 1) * 1024), tag=f"wocast{r}")
    for vec, slot in [("a_bo", 3), ("a_gamma", 4), ("a_beta", 5)]:
        prep_direct_bias(e, vec, slot, tag=f"db{vec}")
    for i, (w1, w2, b1, b2) in enumerate([
            ("a_Wq", "a_Wq2", "a_bq", "a_bq2"),
            ("a_Wk", "a_Wk2", "a_bk", "a_bk2"),
            ("a_Wv", "a_Wv2", "a_bv", "a_bv2")]):
        prep_fused_pair(e, w1, w2, b1, b2, i)


def prep_fused_pair(e, w1, w2, b1, b2, slot):
    for r in range(8):
        st_ap, st_k = dma_slot(e, 2)
        e.DMA("sync", lambda g, r=r, st_ap=st_ap, w2=w2: g.dma_start(
            out=st_ap, in_=e.W[w2][r * 128:(r + 1) * 128, :]),
            w=st_k, tag=f"{w2}ld{r}")
        e.V(lambda v, r=r, st_ap=st_ap: v.tensor_copy(
            e.vprime[:, r * 1024:(r + 1) * 1024], st_ap),
            r=st_k, w=vpk(r * 1024, (r + 1) * 1024), tag=f"{w2}c{r}")
    for kc in range(8):
        st_ap, st_k = dma_slot(e, 2)
        e.DMA("sync", lambda g, kc=kc, st_ap=st_ap, w1=w1: g.dma_start(
            out=st_ap, in_=e.W[w1][kc * 128:(kc + 1) * 128, :]),
            w=st_k, tag=f"{w1}ld{kc}")
        tmp = e.lnt[:, 0:1024]
        e.V(lambda v, st_ap=st_ap, tmp=tmp: v.tensor_copy(tmp, st_ap),
            r=st_k, w=[("lnt", 0), ("lnt", 1)], tag=f"{w1}c{kc}")
        dst3d = e.vprime[:, 8192:16384].rearrange(
            "p (m t) -> p m t", m=8)[:, :, kc * 128:(kc + 1) * 128]
        transpose_1024(e, tmp, [("lnt", 0), ("lnt", 1)], dst3d,
                       vpk(8192, 16384), e.tpA, "tpA", tag=f"{w1}T{kc}")
    for kc in range(8):
        for jh in range(2):
            bi = (kc * 2 + jh) % 3
            bank = e.pool[bi]
            for mc in range(8):
                e.T(lambda t, kc=kc, jh=jh, mc=mc, bank=bank: t.matmul(
                    bank[:],
                    e.vprime[:, 8192 + mc * 1024 + kc * 128:
                             8192 + mc * 1024 + (kc + 1) * 128],
                    e.vprime[:, mc * 1024 + jh * 512:mc * 1024 + (jh + 1) * 512],
                    start=(mc == 0), stop=(mc == 7)),
                    r=vpk(8192 + mc * 1024, 8192 + (mc + 1) * 1024) +
                    vpk(mc * 1024, (mc + 1) * 1024),
                    w=[("pool", bi)], tag=f"fus{slot}")
            e.S(lambda s, kc=kc, jh=jh, bank=bank, slot=slot, bi=bi: s.copy(
                e.wreg[:, slot * 8192 + kc * 1024 + jh * 512:
                       slot * 8192 + kc * 1024 + (jh + 1) * 512], bank[:]),
                r=[("pool", bi)], w=[("wreg", slot, kc, jh)], tag=f"fusc{slot}")
    rf_ap, rf_k = dma_slot(e, 2)
    rowf = rf_ap[0:1, 0:1024]
    e.DMA("sync", lambda g, rowf=rowf, b1=b1: g.dma_start(
        out=rowf, in_=e.W[b1].rearrange("(a b) -> a b", a=1)),
        w=rf_k, tag=f"ld{b1}")
    rowb = e.lnt[0:1, 0:1024]
    e.V(lambda v, rowf=rowf, rowb=rowb: v.tensor_copy(rowb, rowf),
        r=rf_k, w=[("lnt", 0), ("lnt", 1)], tag=f"c{b1}")
    for c in range(8):
        e.T(lambda t, c=c, rowb=rowb: t.transpose(
            e.tpA[:, 2 * c:2 * c + 1], rowb[0:1, c * 128:(c + 1) * 128],
            e.misc[0:1, 0:1]),
            r=[("lnt", 0), ("lnt", 1), ("misc", "id")], w=[("tpA",)], tag="bfT")
    e.S(lambda s: s.copy(
        e.k2t[:, 0:8].rearrange("p (c x) -> p c x", x=1),
        e.tpA[:, 0:16].rearrange("p (c x) -> p c x", x=2)[:, :, 0:1]),
        r=[("tpA",)], w=[("k2t", 0)], tag="bfTc")
    rf2_ap, rf2_k = scr_slot(e, 2)
    rowf2 = rf2_ap[0:1, 0:1024]
    e.DMA("sync", lambda g, rowf2=rowf2, b2=b2: g.dma_start(
        out=rowf2, in_=e.W[b2].rearrange("(a b) -> a b", a=1)),
        w=rf2_k, tag=f"ld{b2}")
    rowbb = e.lnf[0:1, 0:1024]
    for h in range(2):
        for mc in range(8):
            e.T(lambda t, h=h, mc=mc: t.matmul(
                e.pool[3][0:1, 0:512], e.k2t[:, mc:mc + 1],
                e.vprime[:, mc * 1024 + h * 512:mc * 1024 + (h + 1) * 512],
                start=(mc == 0), stop=(mc == 7)),
                r=[("k2t", 0)] + vpk(mc * 1024, (mc + 1) * 1024),
                w=[("pool", 3)], tag="bfmm")
        rs_ap, rs_k = scr_slot(e, 1)
        rsum = rs_ap[0:1, 0:512]
        e.V(lambda v, h=h, rsum=rsum, rowf2=rowf2: v.tensor_tensor(
            rsum, e.pool[3][0:1, 0:512], rowf2[0:1, h * 512:(h + 1) * 512], ALU.add),
            r=[("pool", 3)] + rf2_k, w=rs_k, tag="bfadd")
        e.V(lambda v, rsum=rsum, rowbb=rowbb: v.tensor_copy(rowbb[0:1, 0:512], rsum),
            r=rs_k, w=[("lnf", 0)], tag="bfcast")
        e.T(lambda t, rowbb=rowbb: t.matmul(
            e.pool[3][:, 0:512], e.misc[0:1, 128:256], rowbb[0:1, 0:512],
            start=True, stop=True),
            r=[("lnf", 0), ("misc", "ones")], w=[("pool", 3)], tag="bfbc")
        e.S(lambda s, h=h, slot=slot: s.copy(
            e.btile[:, slot * 1024 + h * 512:slot * 1024 + (h + 1) * 512],
            e.pool[3][:, 0:512]),
            r=[("pool", 3)], w=[("bt", slot, h)], tag="bfbcc")


def quant_window(e, src_ap, src_keys, w, dst_ap, dst_keys, bufs, tag="q"):
    sc_ap, sck, rs_ap, rsk = qsc(e, w // 16)
    src3 = src_ap.rearrange("p (b k) -> p b k", k=16)
    e.V(lambda v: v.tensor_reduce(sc_ap, src3, AX.X, ALU.max,
                                  apply_absolute_value=True),
        r=list(src_keys), w=[sck], tag=tag + "mx")
    e.V(lambda v: v.tensor_scalar(sc_ap, sc_ap, 1.0 / 6.0, 1e-12, ALU.mult, ALU.max),
        r=[sck], w=[sck], tag=tag + "sc")
    e.V(lambda v: v.reciprocal(rs_ap, sc_ap), r=[sck], w=[rsk], tag=tag + "rc")
    xn, xn_k = bufs["xn"]
    c1, c1_k = bufs["c1"]
    tb, tb_k = bufs["t"]
    e.V(lambda v: v.tensor_tensor(xn.rearrange("p (b k) -> p b k", k=16), src3,
                                  bcast_inner(rs_ap, 16), ALU.mult),
        r=list(src_keys) + [rsk], w=xn_k, tag=tag + "xn")
    e.V(lambda v: v.tensor_scalar(c1, xn, 2.0, -2.0, ALU.min, ALU.max),
        r=xn_k, w=c1_k, tag=tag + "c1")
    e.V(lambda v: v.tensor_tensor(tb, xn, c1, ALU.subtract),
        r=xn_k + c1_k, w=tb_k, tag=tag + "t")
    r1, r1_k = bufs["r1"]
    e.V(lambda v: v.tensor_scalar(r1, c1, C05, C05, ALU.add, ALU.subtract),
        r=c1_k, w=r1_k, tag=tag + "r1")
    e.V(lambda v: v.tensor_scalar(c1, tb, 2.0, -2.0, ALU.min, ALU.max),
        r=tb_k, w=c1_k, tag=tag + "d2")
    e.V(lambda v: v.tensor_tensor(xn, tb, c1, ALU.subtract),
        r=tb_k + c1_k, w=xn_k, tag=tag + "d3")
    r2, r2_k = bufs["r2"]
    e.V(lambda v: v.tensor_scalar(r2, c1, C1F, C1F, ALU.add, ALU.subtract),
        r=c1_k, w=r2_k, tag=tag + "r2")
    r3, r3_k = bufs["r3"]
    e.V(lambda v: v.tensor_scalar(r3, xn, C2F, C2F, ALU.add, ALU.subtract),
        r=xn_k, w=r3_k, tag=tag + "r3")
    g1, g1_k = bufs["g1"]
    e.V(lambda v: v.tensor_tensor(g1, r1, r2, ALU.add),
        r=r1_k + r2_k, w=g1_k, tag=tag + "g1")
    g, g_k = bufs["g"]
    e.V(lambda v: v.tensor_tensor(g, g1, r3, ALU.add),
        r=g1_k + r3_k, w=g_k, tag=tag + "g")
    e.V(lambda v: v.tensor_tensor(dst_ap.rearrange("p (b k) -> p b k", k=16),
                                  g.rearrange("p (b k) -> p b k", k=16),
                                  bcast_inner(sc_ap, 16), ALU.mult),
        r=g_k + [sck], w=list(dst_keys), tag=tag + "qq")


def bf16_bufs(e, w, par=0):
    if w > 512:
        return {
            "r1": (e.q2t[:, 0:w], [("q2t", 0), ("q2t", 1)]),
            "r2": (e.k2t[:, 0:w], [("k2t", 0), ("k2t", 1)]),
            "r3": (e.lnf[:, 0:w], [("lnf", 0), ("lnf", 1)]),
            "g1": (e.sbd[:, 0:w], [("sbd", i) for i in range(8)]),
            "g": (e.lnt[:, 1024:1024 + w], [("lnt", 2), ("lnt", 3)]),
        }
    o = par * 512
    return {
        "r1": (e.q2t[:, o:o + w], [("q2t", par)]),
        "r2": (e.k2t[:, o:o + w], [("k2t", par)]),
        "r3": (e.lnf[:, o:o + w], [("lnf", par)]),
        "g1": (e.sbd[:, o:o + w], [("sbd", i + 4 * par) for i in range(4)]),
        "g": (e.lnt[:, 1024 + o:1024 + o + w], [("lnt", 2 + par)]),
    }


def prep_ffn_weights(e):
    import os
    nwg = int(os.environ.get("KNWG", "22"))
    nwu = int(os.environ.get("KNWU", "22"))
    nwd = int(os.environ.get("KNWD", "8"))
    for r in range(nwg):
        prep_ffn_wgwu_tile(e, "wg", e.W["f_wg"], e.wgq, r)
    for r in range(nwu):
        prep_ffn_wgwu_tile(e, "wu", e.W["f_wu"], e.wuq, r)
    for r in range(nwd):
        prep_ffn_wd_tile(e, r)


PREPQ = [0]


def _quant_and_store_windows(e, wname, dram_dst, r, st_ap, base_col, nwcols,
                             stage_keys, win=512):
    nwin = (nwcols + win - 1) // win
    for hw in range(nwin):
        w0 = hw * win
        ww = min(nwcols, w0 + win) - w0
        sw = st_ap[:, w0:w0 + ww]
        swk = [stage_keys[i] for i in range(w0 // 512, (w0 + ww + 511) // 512)]
        ns = max(1, ww // 512)
        xn, xn_k = scr_slot(e, ns)
        c1a, c1_k = scr_slot(e, ns)
        ta, t_k = scr_slot(e, ns)
        PREPQ[0] ^= 1
        bufs = bf16_bufs(e, ww, PREPQ[0])
        bufs["xn"] = (xn[:, 0:ww], xn_k)
        bufs["c1"] = (c1a[:, 0:ww], c1_k)
        bufs["t"] = (ta[:, 0:ww], t_k)
        if ww > 512:
            par = 0
            dst_bf = e.lnt[:, 0:ww]
            dkeys = [("lnt", 0), ("lnt", 1)]
        else:
            par = PREPQ[0]
            dst_bf = e.lnt[:, par * 512:par * 512 + ww]
            dkeys = [("lnt", par)]
        quant_window(e, sw, swk, ww, dst_bf, dkeys, bufs,
                     tag=f"{wname}q{r}_{base_col + w0}")
        for ci in range(ww // 128):
            c = (base_col + w0) // 128 + ci
            tp, tpn = (e.tpA, "tpA") if (c % 2 == 0) else (e.tpB, "tpB")
            sl = c % 8
            e.T(lambda t, ci=ci, tp=tp, sl=sl, dst_bf=dst_bf: t.transpose(
                tp[:, sl * 128:(sl + 1) * 128], dst_bf[:, ci * 128:(ci + 1) * 128],
                e.misc[:, 0:128]),
                r=dkeys + [("misc", "id")], w=[("tpA",) if tp is e.tpA else ("tpB",)], tag=f"{wname}T")
            stg = e.misc[:, 256 + (c % 2) * 128:256 + (c % 2) * 128 + 128]
            e.S(lambda s, tp=tp, sl=sl, stg=stg: s.copy(
                stg, tp[:, sl * 128:(sl + 1) * 128]),
                r=[("tpA",) if tp is e.tpA else ("tpB",)], w=[("wstg", c % 2)], tag=f"{wname}Tc")
            e.DMA("gpsimd", lambda g, c=c, r=r, stg=stg, dram_dst=dram_dst:
                  g.dma_start(out=dram_dst[c * 128:(c + 1) * 128,
                                           r * 128:(r + 1) * 128], in_=stg),
                  r=[("wstg", c % 2)], w=[("wqscr", wname)], tag=f"{wname}st")


def prep_ffn_wgwu_tile(e, wname, dram_src, dram_dst, r):
    st_ap, st_k = dma_slot(e, 2)
    e.DMA("sync", lambda g, r=r, st_ap=st_ap, dram_src=dram_src: g.dma_start(
        out=st_ap, in_=dram_src[r * 128:(r + 1) * 128, :]),
        w=st_k, tag=f"{wname}ld{r}")
    _quant_and_store_windows(e, wname, dram_dst, r, st_ap, 0, 1024, st_k, win=1024)


def prep_ffn_wd_tile(e, r):
    old_lo = e.scr_lo
    e.scr_lo = 3
    st_ap = e.scr[:, 0:1536]
    st_k = [("scr", 0), ("scr", 1), ("scr", 2)]
    for half in range(2):
        c0 = half * 1408
        e.DMA("sync", lambda g, r=r, c0=c0, st_ap=st_ap: g.dma_start(
            out=st_ap[:, 0:1408],
            in_=e.W["f_wd"][r * 128:(r + 1) * 128, c0:c0 + 1408]),
            w=st_k, tag=f"wdld{r}_{half}")
        _quant_and_store_windows(e, "wd", e.wdq, r, st_ap, c0, 1408, st_k)
    e.scr_lo = old_lo


def prep_plain_transposed(e, src_name, dst_base, dst_keyf, scale=None, tag="wT"):
    for r in range(8):
        st_ap, st_k = dma_slot(e, 2)
        e.DMA("sync", lambda g, r=r, st_ap=st_ap, src_name=src_name: g.dma_start(
            out=st_ap, in_=e.W[src_name][r * 128:(r + 1) * 128, :]),
            w=st_k, tag=f"{tag}ld{r}")
        tmp = e.wbuf[:, 8192:9216]
        tmp_k = wbk(8192, 9216)
        if scale is None:
            e.V(lambda v, st_ap=st_ap, tmp=tmp: v.tensor_copy(tmp, st_ap),
                r=st_k, w=tmp_k, tag=f"{tag}c{r}")
        else:
            e.V(lambda v, st_ap=st_ap, tmp=tmp, scale=scale: v.tensor_scalar(
                tmp, st_ap, scale, None, ALU.mult),
                r=st_k, w=tmp_k, tag=f"{tag}c{r}")
        dst3d = dst_base.rearrange("p (c t) -> p c t", c=8)[:, :, r * 128:(r + 1) * 128]
        transpose_1024(e, tmp, tmp_k, dst3d, dst_keyf(r), e.tpB, "tpB",
                       tag=f"{tag}T{r}")


def prep_attn_b_kv_weights(e):
    prep_plain_transposed(
        e, "b_Wk", e.wreg[:, 0:8192],
        lambda r: [("wreg", 0, c, r // 4) for c in range(8)],
        scale=SCL_B, tag="bWk")
    prep_plain_transposed(
        e, "b_Wv", e.wreg[:, 8192:16384],
        lambda r: [("wreg", 1, c, r // 4) for c in range(8)],
        tag="bWv")
    prep_direct_bias(e, "b_bv", 0, tag="dbbv")
    prep_direct_bias(e, "b_bo", 1, tag="dbbo")
    rf_ap, rf_k = dma_slot(e, 2)
    rowf = rf_ap[0:1, 0:1024]
    e.DMA("sync", lambda g, rowf=rowf: g.dma_start(
        out=rowf, in_=e.W["b_bq"].rearrange("(a b) -> a b", a=1)),
        w=rf_k, tag="rbq")
    rowb = e.lnf[0:1, 0:1024]
    e.V(lambda v, rowf=rowf, rowb=rowb: v.tensor_scalar(
        rowb, rowf, SCL_B, None, ALU.mult),
        r=rf_k, w=[("lnf", 0)], tag="rcbq")
    for c in range(8):
        e.T(lambda t, c=c, rowb=rowb: t.transpose(
            e.tpA[:, 2 * c:2 * c + 1], rowb[0:1, c * 128:(c + 1) * 128],
            e.misc[0:1, 0:1]),
            r=[("lnf", 0), ("misc", "id")], w=[("tpA",)], tag="bqbT")
    e.S(lambda s: s.copy(
        e.smf[:, 336:344].rearrange("p (c x) -> p c x", x=1),
        e.tpA[:, 0:16].rearrange("p (c x) -> p c x", x=2)[:, :, 0:1]),
        r=[("tpA",)], w=[("bqcol",)], tag="bqbTc")


def prep_attn_b_qo_weights(e):
    prep_plain_transposed(
        e, "b_Wq", e.wreg[:, 16384:24576],
        lambda r: [("wreg", 2, c, r // 4) for c in range(8)],
        scale=SCL_B, tag="bWq")
    prep_plain_transposed(
        e, "b_Wo", e.vprime[:, 0:8192],
        lambda r: [("vpk", (2 * c * 512 + r * 128) // 512) for c in range(8)],
        tag="bWo")



def stats_block(e, b):
    sq, sq_k = scr_slot(e, 2)
    e.S(lambda s, b=b, sq=sq: s.activation(
        sq, e.stream[:, b * 1024:(b + 1) * 1024], AF.Square),
        r=[("st", b)], w=sq_k, tag=f"stsq{b}")
    e.V(lambda v, b=b, sq=sq: v.tensor_reduce(
        e.sta[:, 96 + b:97 + b], sq, AX.X, ALU.add),
        r=sq_k, w=[("m2r", b)], tag=f"stm2{b}")
    e.V(lambda v, b=b: v.tensor_reduce(
        e.sta[:, 64 + b:65 + b], e.stream[:, b * 1024:(b + 1) * 1024],
        AX.X, ALU.add),
        r=[("st", b)], w=[("mur", b)], tag=f"stmu{b}")


def stats_agg(e):
    mu = e.sta[:, 64:96]
    m2 = e.sta[:, 96:128]
    rstd = e.sta[:, 0:32]
    nmr = e.sta[:, 32:64]
    mall = [("mur", b) for b in range(NBLK)] + [("m2r", b) for b in range(NBLK)]
    e.V(lambda v, mu=mu: v.tensor_scalar(mu, mu, 1.0 / 1024.0, None, ALU.mult),
        r=mall, w=[("muN",)], tag="stmun")
    e.V(lambda v, m2=m2: v.tensor_scalar(m2, m2, 1.0 / 1024.0, None, ALU.mult),
        r=mall, w=[("m2N",)], tag="stm2n")
    tmp = e.smf[:, 304:336]
    e.V(lambda v, mu=mu, tmp=tmp: v.tensor_tensor(tmp, mu, mu, ALU.mult),
        r=[("muN",)], w=[("mu2",)], tag="stmu2")
    e.V(lambda v, m2=m2, tmp=tmp: v.tensor_tensor(m2, m2, tmp, ALU.subtract),
        r=[("m2N",), ("mu2",)], w=[("var",)], tag="stvar")
    e.S(lambda s, rstd=rstd, m2=m2: s.activation(
        rstd, m2, AF.Ln, bias=EPS_OUT, scale=1.0),
        r=[("var",), ("cpk",)], w=[("lnv",)], tag="stln")
    e.S(lambda s, rstd=rstd: s.activation(rstd, rstd, AF.Exp, bias=0.0, scale=-0.5),
        r=[("lnv",), ("cpk",)], w=[("rstd",)], tag="stexp")
    e.V(lambda v, mu=mu, rstd=rstd, nmr=nmr: v.scalar_tensor_tensor(
        nmr, mu, -1.0, rstd, ALU.mult, ALU.mult),
        r=[("muN",), ("rstd",)], w=[("nmr",)], tag="stnmr")


def ln_apply(e, b, dst_ap, dst_keys, tag="ln"):
    return e.S(lambda s, b=b, dst_ap=dst_ap: s.activation(
        dst_ap, e.stream[:, b * 1024:(b + 1) * 1024], AF.Identity,
        bias=e.sta[:, 32 + b:33 + b], scale=e.sta[:, b:b + 1]),
        r=[("st", b), ("rstd",), ("nmr",)], w=list(dst_keys), tag=f"{tag}{b}")



def ofm_slot(e, g):
    if g < 2:
        return e.q2t[:, (g % 2) * 512:(g % 2) * 512 + 512], ("q2t", g % 2)
    if g < 4:
        return e.k2t[:, (g % 2) * 512:(g % 2) * 512 + 512], ("k2t", g % 2)
    return e.lnt[:, (g - 4) * 512:(g - 4) * 512 + 512], ("lnt", g - 4)


def attn_a(e):
    prot = [0]

    def nxt_pool():
        prot[0] = (prot[0] + 1) % 4
        return prot[0], e.pool[prot[0]], ("pool", prot[0])

    for seq in range(2):
        e.p.scope = "attnA_pA"
        e.V(lambda v: v.memset(e.acc[:], 0.0),
            w=[("accA",), ("accB",)], tag="accz")
        pend_vT = [None]
        ln_apply(e, seq * BPS, e.lnt[:, 0:1024], [("lnt", 0), ("lnt", 1)],
                 tag="alna")
        for bt in range(BPS):
            gb = seq * BPS + bt
            buf = bt % 2
            lnt_ap = e.lnt[:, buf * 1024:(buf + 1) * 1024]
            if buf == 0:
                lnf_ap = e.lnf[:, 0:1024]
                lnf_keys = [("lnf", 0), ("lnf", 1)]
                lnf_rd = lambda kc: ("lnf", kc // 4)
            else:
                lnf_ap = e.sbd[:, 0:1024]
                lnf_keys = [("sbd", i) for i in range(8)]
                lnf_rd = lambda kc: ("sbd", kc)
            tp, tpn = (e.tpA, "tpA") if buf == 0 else (e.tpB, "tpB")
            transpose_1024(e, lnt_ap, [("lnt", 2 * buf), ("lnt", 2 * buf + 1)],
                           lnf_ap.rearrange("p (c x) -> p c x", x=128),
                           lnf_keys, tp, tpn, tag=f"alnT{gb}")
            for wslot, dstap, dstk in [(0, e.q2t, "q2t"), (1, e.k2t, "k2t")]:
                for h in range(2):
                    bi, bank, bkey = nxt_pool()
                    for kc in range(8):
                        e.T(lambda t, kc=kc, h=h, wslot=wslot, bank=bank,
                            lnf_ap=lnf_ap: t.matmul(
                                bank[:], lnf_ap[:, kc * 128:(kc + 1) * 128],
                                e.wreg[:, wslot * 8192 + kc * 1024 + h * 512:
                                       wslot * 8192 + kc * 1024 + (h + 1) * 512],
                                start=(kc == 0), stop=(kc == 7)),
                            r=[lnf_rd(kc), ("wreg", wslot, kc, h)], w=[bkey],
                            tag="aqk")
                    a_ap, a_k = scr_slot(e, 1)
                    m_ap, m_k = scr_slot(e, 1)
                    e.V(lambda v, bank=bank, a_ap=a_ap, h=h, wslot=wslot,
                        bkey=bkey: v.scalar_tensor_tensor(
                            a_ap, bank[:], 0.0,
                            e.btile[:, wslot * 1024 + h * 512:
                                    wslot * 1024 + (h + 1) * 512],
                            ALU.bypass, ALU.add),
                        r=[bkey, ("bt", wslot, h)], w=a_k, tag="abias")
                    e.V(lambda v, a_ap=a_ap, m_ap=m_ap: v.tensor_scalar(
                        m_ap, a_ap, 0.0, None, ALU.min),
                        r=a_k, w=m_k, tag="amin")
                    e.S(lambda s, m_ap=m_ap: s.activation(m_ap, m_ap, AF.Exp),
                        r=m_k + [("cpk",)], w=m_k, tag="aexp")
                    e.V(lambda v, a_ap=a_ap, m_ap=m_ap, dstap=dstap, h=h:
                        v.scalar_tensor_tensor(
                            dstap[:, h * 512:(h + 1) * 512], a_ap, 0.0, m_ap,
                            ALU.max, ALU.add),
                        r=a_k + m_k, w=[(dstk, h)], tag="aelu")
            if pend_vT[0] is not None:
                pend_vT[0]()
                pend_vT[0] = None
            if bt + 1 < BPS:
                nb = (bt + 1) % 2
                ln_apply(e, gb + 1, e.lnt[:, nb * 1024:(nb + 1) * 1024],
                         [("lnt", 2 * nb), ("lnt", 2 * nb + 1)], tag="alna")
            prod, prod_k = scr_slot(e, 2)
            e.V(lambda v, prod=prod: v.tensor_tensor(prod, e.q2t[:], e.k2t[:],
                                                     ALU.mult),
                r=[("q2t", 0), ("q2t", 1), ("k2t", 0), ("k2t", 1)], w=prod_k,
                tag="aqk2")
            qks = e.smf[:, 0:16]
            rq = e.smf[:, 16:32]
            e.V(lambda v, prod=prod, qks=qks: v.tensor_reduce(
                qks, prod.rearrange("p (hh k) -> p hh k", k=64), AX.X, ALU.add),
                r=prod_k, w=[("qks",)], tag="aqks")
            e.V(lambda v, qks=qks: v.tensor_scalar(qks, qks, EPS_QK, None, ALU.add),
                r=[("qks",)], w=[("qks",)], tag="aqke")
            e.V(lambda v, qks=qks, rq=rq: v.reciprocal(rq, qks),
                r=[("qks",)], w=[("rq",)], tag="arq")
            for g in range(8):
                e.T(lambda t, g=g, bt=bt: t.matmul(
                    e.acc[:, g * 128:(g + 1) * 128],
                    e.k2t[:, g * 128:(g + 1) * 128],
                    e.q2t[:, g * 128:(g + 1) * 128],
                    start=False, stop=(bt == BPS - 1), skip_group_check=True),
                    r=[("q2t", g // 4), ("k2t", g // 4)],
                    w=[("accA",) if g < 4 else ("accB",)], tag="aS")
            vt, vt_k = scr_slot(e, 2)
            for h in range(2):
                bi, bank, bkey = nxt_pool()
                for kc in range(8):
                    e.T(lambda t, kc=kc, h=h, bank=bank, lnf_ap=lnf_ap: t.matmul(
                        bank[:], lnf_ap[:, kc * 128:(kc + 1) * 128],
                        e.wreg[:, 2 * 8192 + kc * 1024 + h * 512:
                               2 * 8192 + kc * 1024 + (h + 1) * 512],
                        start=(kc == 0), stop=(kc == 7)),
                        r=[lnf_rd(kc), ("wreg", 2, kc, h)], w=[bkey], tag="av2")
                e.V(lambda v, bank=bank, h=h, vt=vt, bkey=bkey:
                    v.scalar_tensor_tensor(
                        vt[:, h * 512:(h + 1) * 512], bank[:], 0.0,
                        e.btile[:, 2 * 1024 + h * 512:2 * 1024 + (h + 1) * 512],
                        ALU.bypass, ALU.add),
                    r=[bkey, ("bt", 2, h)], w=[vt_k[h]], tag="avb")
            e.V(lambda v, vt=vt, lnt_ap=lnt_ap: v.tensor_tensor(
                lnt_ap.rearrange("p (hh k) -> p hh k", k=64),
                vt.rearrange("p (hh k) -> p hh k", k=64),
                bcast_inner(e.smf[:, 16:32], 64), ALU.mult),
                r=vt_k + [("rq",)], w=[("lnt", 2 * buf), ("lnt", 2 * buf + 1)], tag="avdiv")
            tp2, tp2n = (e.tpB, "tpB") if buf == 0 else (e.tpA, "tpA")
            vdst = e.vprime[:].rearrange("p (c t) -> p c t", c=8)[
                :, :, bt * 128:(bt + 1) * 128]
            def _vt(lnt_ap=lnt_ap, buf=buf, vdst=vdst, tp2=tp2, tp2n=tp2n,
                    bt=bt, gb=gb):
                transpose_1024(e, lnt_ap, [("lnt", 2 * buf), ("lnt", 2 * buf + 1)], vdst,
                               [("vpk", 4 * c + bt // 4) for c in range(8)],
                               tp2, tp2n, tag=f"avT{gb}")
            pend_vT[0] = _vt
        if pend_vT[0] is not None:
            pend_vT[0]()
            pend_vT[0] = None
        e.p.scope = "attnA_ev"
        e.V(lambda v: v.memset(e.sbd[:], 0.0),
            w=[("sbd", g) for g in range(8)], tag="sbz")
        for g in range(8):
            e.S(lambda s, g=g: s.copy(
                e.sbd[0:64, g * 128:g * 128 + 64],
                e.acc[0:64, g * 128:g * 128 + 64]),
                r=[("accA",) if g < 4 else ("accB",), ("sbd", g)], w=[("sbd", g)], tag="sev0")
            e.S(lambda s, g=g: s.copy(
                e.sbd[64:128, g * 128 + 64:g * 128 + 128],
                e.acc[64:128, g * 128 + 64:g * 128 + 128]),
                r=[("accA",) if g < 4 else ("accB",), ("sbd", g)], w=[("sbd", g)], tag="sev1")
        e.p.scope = "attnA_pB"
        banks6 = [(e.pool[i][:], ("pool", i)) for i in range(4)]
        prot6 = [0]

        def nxt_pool6():
            prot6[0] = (prot6[0] + 1) % 4
            ap, key = banks6[prot6[0]]
            return prot6[0], ap, key
        for sp in range(NSPAN):
            m0 = sp * 512
            for g in range(8):
                bi, bank, bkey = nxt_pool6()
                e.T(lambda t, g=g, m0=m0, bank=bank: t.matmul(
                    bank, e.sbd[:, g * 128:(g + 1) * 128],
                    e.vprime[:, g * 2048 + m0:g * 2048 + m0 + 512],
                    start=True, stop=True),
                    r=[("sbd", g), ("vpk", 4 * g + sp)], w=[bkey], tag="aob")
                oap, okey = ofm_slot(e, g)
                e.S(lambda s, bank=bank, oap=oap, bkey=bkey: s.copy(oap, bank),
                    r=[bkey], w=[okey], tag="aoc")
            for mb in range(4):
                gb = seq * BPS + sp * 4 + mb
                y = e.scr[:, 1024:2048]
                y_k = [("scr", 2), ("scr", 3)]
                for h in range(2):
                    bi, bank, bkey = nxt_pool6()
                    for ec in range(8):
                        oap, okey = ofm_slot(e, ec)
                        e.T(lambda t, ec=ec, h=h, mb=mb, bank=bank, oap=oap:
                            t.matmul(
                                bank, oap[:, mb * 128:(mb + 1) * 128],
                                e.wbuf[:, ec * 1024 + h * 512:
                                       ec * 1024 + (h + 1) * 512],
                                start=(ec == 0), stop=(ec == 7)),
                            r=[okey] + wbk(ec * 1024, (ec + 1) * 1024),
                            w=[bkey], tag="awo")
                    e.V(lambda v, bank=bank, h=h, y=y, bkey=bkey:
                        v.scalar_tensor_tensor(
                            y[:, h * 512:(h + 1) * 512], bank, 0.0,
                            e.btile[:, 3 * 1024 + h * 512:3 * 1024 + (h + 1) * 512],
                            ALU.bypass, ALU.add),
                        r=[bkey, ("bt", 3, h)], w=[y_k[h]], tag="ayb")
                lnr = e.scr[:, 2048:3072]
                lnr_k = [("scr", 4), ("scr", 5)]
                ln_apply(e, gb, lnr, lnr_k, tag="alnr")
                e.V(lambda v, y=y, lnr=lnr: v.tensor_tensor(y, y, lnr, ALU.add),
                    r=y_k + lnr_k, w=y_k, tag="ay")
                sq = e.scr[:, 3072:4096]
                sq_k = [("scr", 6), ("scr", 7)]
                e.S(lambda s, y=y, sq=sq: s.activation(sq, y, AF.Square),
                    r=y_k, w=sq_k, tag="aisq")
                mu1 = e.smf[:, 32:33]
                m21 = e.smf[:, 33:34]
                t1 = e.smf[:, 34:35]
                r2d = e.smf[:, 35:36]
                nm2 = e.smf[:, 36:37]
                e.V(lambda v, y=y, mu1=mu1: v.tensor_reduce(mu1, y, AX.X, ALU.add),
                    r=y_k, w=[("imu",)], tag="aimu")
                e.V(lambda v, sq=sq, m21=m21: v.tensor_reduce(m21, sq, AX.X, ALU.add),
                    r=sq_k, w=[("im2",)], tag="aim2")
                e.V(lambda v, mu1=mu1: v.tensor_scalar(
                    mu1, mu1, 1.0 / 1024.0, None, ALU.mult),
                    r=[("imu",)], w=[("imu",)], tag="aimun")
                e.V(lambda v, m21=m21: v.tensor_scalar(
                    m21, m21, 1.0 / 1024.0, None, ALU.mult),
                    r=[("im2",)], w=[("im2",)], tag="aim2n")
                e.V(lambda v, mu1=mu1, t1=t1: v.tensor_tensor(t1, mu1, mu1, ALU.mult),
                    r=[("imu",)], w=[("imusq",)], tag="aimu2")
                e.V(lambda v, m21=m21, t1=t1: v.tensor_tensor(
                    m21, m21, t1, ALU.subtract),
                    r=[("im2",), ("imusq",)], w=[("ivar",)], tag="aivar")
                e.S(lambda s, r2d=r2d, m21=m21: s.activation(
                    r2d, m21, AF.Ln, bias=EPS_IN, scale=1.0),
                    r=[("ivar",), ("cpk",)], w=[("ilnv",)], tag="ailn")
                e.S(lambda s, r2d=r2d: s.activation(
                    r2d, r2d, AF.Exp, bias=0.0, scale=-0.5),
                    r=[("ilnv",), ("cpk",)], w=[("irstd",)], tag="aiex")
                e.V(lambda v, mu1=mu1, r2d=r2d, nm2=nm2: v.scalar_tensor_tensor(
                    nm2, mu1, -1.0, r2d, ALU.mult, ALU.mult),
                    r=[("imu",), ("irstd",)], w=[("inmr",)], tag="ainm")
                z = e.scr[:, 2048:3072]
                z_k = [("scr", 4), ("scr", 5)]
                e.S(lambda s, y=y, z=z, r2d=r2d, nm2=nm2: s.activation(
                    z, y, AF.Identity, bias=nm2, scale=r2d),
                    r=y_k + [("irstd",), ("inmr",)], w=z_k, tag="aiz")
                e.P(lambda v, z=z: v.tensor_tensor(
                    z, z, e.btile[:, 4 * 1024:5 * 1024], ALU.mult),
                    r=z_k + [("bt", 4, 0), ("bt", 4, 1)], w=z_k, tag="aig")
                e.P(lambda v, z=z: v.tensor_tensor(
                    z, z, e.btile[:, 5 * 1024:6 * 1024], ALU.add),
                    r=z_k + [("bt", 5, 0), ("bt", 5, 1)], w=z_k, tag="aib")
                e.P(lambda v, z=z, gb=gb: v.tensor_tensor(
                    e.stream[:, gb * 1024:(gb + 1) * 1024],
                    z, e.stream[:, gb * 1024:(gb + 1) * 1024], ALU.add),
                    r=z_k + [("st", gb)], w=[("st", gb)], tag="ares")
                stats_block(e, gb)



def ffn_xquant(e, mac):
    for tb in range(4):
        ffn_xquant_tb(e, mac, tb)


def ffn_xquant_tb(e, mac, tb):
    if True:
        gb = mac * 4 + tb
        lnq, lnq_k = scr_slot(e, 2)
        ln_apply(e, gb, lnq, lnq_k, tag="flna")
        xn, xn_k = scr_slot(e, 2)
        c1a, c1_k = scr_slot(e, 2)
        ta, t_k = scr_slot(e, 2)
        bufs = bf16_bufs(e, 1024)
        bufs["xn"] = (xn, xn_k)
        bufs["c1"] = (c1a, c1_k)
        bufs["t"] = (ta, t_k)
        dst_bf = e.lnt[:, 0:1024]
        quant_window(e, lnq, lnq_k, 1024, dst_bf, [("lnt", 0), ("lnt", 1)],
                     bufs, tag=f"fxq{gb}")
        xdst = e.vprime[:, 0:4096].rearrange(
            "p (c t) -> p c t", c=8)[:, :, tb * 128:(tb + 1) * 128]
        tp, tpn = (e.tpA, "tpA") if tb % 2 == 0 else (e.tpB, "tpB")
        transpose_1024(e, dst_bf, [("lnt", 0), ("lnt", 1)], xdst,
                       [("vpk", c) for c in range(8)], tp, tpn,
                       tag=f"fxT{gb}")


def ffn_alpha(e, mac, nxt_pool):
    pend = []
    for j, (n0, nw) in enumerate(NCHUNKS):
        jb = j % 2
        wg_ap = e.wbuf[:, jb * 4096:(jb + 1) * 4096]
        wu_ap = e.wreg[:, 2 * 8192 + jb * 4096:2 * 8192 + (jb + 1) * 4096]
        wg_keys = wbk(jb * 4096, (jb + 1) * 4096)
        wu_keys = wreg_keys(2, 4 * jb, 4 * jb + 4, 0, 2)
        e.DMA("sync", lambda g, n0=n0, nw=nw, wg_ap=wg_ap: g.dma_start(
            out=wg_ap.rearrange("p (kc n) -> p kc n", n=512)[:, :, 0:nw],
            in_=e.wgq.rearrange("(kc p) n -> p kc n", p=128)[:, :, n0:n0 + nw]),
            r=[("wqscr", "wg")], w=wg_keys, tag=f"fwg{mac}_{j}")
        e.DMA("sync", lambda g, n0=n0, nw=nw, wu_ap=wu_ap: g.dma_start(
            out=wu_ap.rearrange("p (kc n) -> p kc n", n=512)[:, :, 0:nw],
            in_=e.wuq.rearrange("(kc p) n -> p kc n", p=128)[:, :, n0:n0 + nw]),
            r=[("wqscr", "wu")], w=wu_keys, tag=f"fwu{mac}_{j}")
        for tb in range(4):
            biG, bankG, bkG = nxt_pool()
            biU, bankU, bkU = nxt_pool()
            for kc in range(8):
                e.T(lambda t, kc=kc, tb=tb, nw=nw, bankG=bankG, wg_ap=wg_ap:
                    t.matmul(
                        bankG[:, 0:nw],
                        e.vprime[:, kc * 512 + tb * 128:kc * 512 + (tb + 1) * 128],
                        wg_ap[:, kc * 512:kc * 512 + nw],
                        start=(kc == 0), stop=(kc == 7)),
                    r=[("vpk", kc)] + wg_keys, w=[bkG], tag="fg")
                e.T(lambda t, kc=kc, tb=tb, nw=nw, bankU=bankU, wu_ap=wu_ap:
                    t.matmul(
                        bankU[:, 0:nw],
                        e.vprime[:, kc * 512 + tb * 128:kc * 512 + (tb + 1) * 128],
                        wu_ap[:, kc * 512:kc * 512 + nw],
                        start=(kc == 0), stop=(kc == 7)),
                    r=[("vpk", kc)] + wu_keys, w=[bkU], tag="fu")
            if len(pend) >= 2:
                pend.pop(0)()
            sg, sg_k = scr_slot(e, 1)
            e.S(lambda s, bankG=bankG, sg=sg, nw=nw, bkG=bkG: s.activation(
                sg[:, 0:nw], bankG[:, 0:nw], AF.Silu),
                r=[bkG, ("cpk",)], w=sg_k, tag="fsilu")
            h_ap, h_k = sg, sg_k
            e.V(lambda v, bankU=bankU, sg=sg, nw=nw, bkU=bkU:
                v.tensor_tensor(
                    sg[:, 0:nw], sg[:, 0:nw], bankU[:, 0:nw], ALU.mult),
                r=sg_k + [bkU], w=h_k, tag="fh")
            xn, xn_k = scr_slot(e, 1)
            c1a, c1_k = scr_slot(e, 1)
            ta, t_k = scr_slot(e, 1)
            par = (j * 4 + tb) % 2
            bufs = bf16_bufs(e, nw, par)
            bufs["xn"] = (xn[:, 0:nw], xn_k)
            bufs["c1"] = (c1a[:, 0:nw], c1_k)
            bufs["t"] = (ta[:, 0:nw], t_k)
            dst_bf = e.lnt[:, par * 512:par * 512 + nw]
            dst_keys = [("lnt", par)]
            quant_window(e, h_ap[:, 0:nw], h_k, nw, dst_bf, dst_keys,
                         bufs, tag=f"fhq{mac}_{j}_{tb}")

            def _ht(j=j, tb=tb, nw=nw, dst_bf=dst_bf, par=par):
                nb = nw // 128
                tp, tpn = (e.tpA, "tpA") if (j % 2 == 0) else (e.tpB, "tpB")
                sl0 = (j * 4) % 8
                for ci in range(nb):
                    e.T(lambda t, ci=ci, tp=tp, sl0=sl0, dst_bf=dst_bf, par=par:
                        t.transpose(
                            tp[:, (sl0 + ci) * 128:(sl0 + ci + 1) * 128],
                            dst_bf[:, ci * 128:(ci + 1) * 128], e.misc[:, 0:128]),
                        r=[("lnt", par), ("misc", "id")], w=[(tpn,)], tag="fhT")
                vdst = e.vprime[:, 4096 + j * 4 * 512:
                                4096 + (j * 4 + nb) * 512].rearrange(
                    "p (c t) -> p c t", t=512)[:, :, tb * 128:(tb + 1) * 128]
                e.S(lambda s, tp=tp, sl0=sl0, nb=nb, vdst=vdst: s.copy(
                    vdst,
                    tp[:, sl0 * 128:(sl0 + nb) * 128].rearrange(
                        "p (c x) -> p c x", x=128)),
                    r=[(tpn,)],
                    w=[("vpk", 8 + j * 4 + ci) for ci in range(nb)], tag="fhTc")
            pend.append(_ht)
    while pend:
        pend.pop(0)()


def ffn_beta_mh(e, mac, mh, final, nxt_pool, hooks=None):
    banks = []
    for tb in range(4):
        bi, bank, bkey = nxt_pool()
        banks.append((bank, bkey))
    for Ic in range(22):
        sl = Ic % 4
        wd_ap = e.wbuf[:, 8192 + sl * 512:8192 + (sl + 1) * 512]
        wd_keys = wbk(8192 + sl * 512, 8192 + (sl + 1) * 512)
        e.DMA("gpsimd", lambda g, Ic=Ic, mh=mh, wd_ap=wd_ap: g.dma_start(
            out=wd_ap,
            in_=e.wdq[Ic * 128:(Ic + 1) * 128, mh * 512:(mh + 1) * 512]),
            r=[("wqscr", "wd")], w=wd_keys, tag=f"fwd{mac}_{mh}_{Ic}")
        for tb in range(4):
            bank, bkey = banks[tb]
            e.T(lambda t, Ic=Ic, tb=tb, bank=bank, wd_ap=wd_ap: t.matmul(
                bank[:, 0:512],
                e.vprime[:, 4096 + Ic * 512 + tb * 128:
                         4096 + Ic * 512 + (tb + 1) * 128],
                wd_ap, start=(Ic == 0), stop=(Ic == 21),
                skip_group_check=True),
                r=[("vpk", 8 + Ic)] + wd_keys, w=[bkey], tag="fd")
        if hooks and Ic in hooks:
            hooks[Ic]()
    for tb in range(4):
        gb = mac * 4 + tb
        bank, bkey = banks[tb]
        if not final:
            e.V(lambda v, bank=bank, gb=gb, mh=mh, bkey=bkey:
                v.tensor_tensor(
                    e.stream[:, gb * 1024 + mh * 512:gb * 1024 + (mh + 1) * 512],
                    bank[:, 0:512],
                    e.stream[:, gb * 1024 + mh * 512:gb * 1024 + (mh + 1) * 512],
                    ALU.add),
                r=[bkey, ("st", gb)], w=[("st", gb)], tag="fres")
            if mh == 1:
                stats_block(e, gb)
        else:
            o_ap, o_k = scr_slot(e, 1)
            e.V(lambda v, bank=bank, gb=gb, mh=mh, o_ap=o_ap, bkey=bkey:
                v.tensor_tensor(
                    o_ap, bank[:, 0:512],
                    e.stream[:, gb * 1024 + mh * 512:gb * 1024 + (mh + 1) * 512],
                    ALU.add),
                r=[bkey, ("st", gb)], w=o_k, tag="fout")
            e.DMA("gpsimd", lambda g, gb=gb, mh=mh, o_ap=o_ap: g.dma_start(
                out=e.out[gb * 128:(gb + 1) * 128, mh * 512:(mh + 1) * 512],
                in_=o_ap),
                r=o_k, w=[("outd", gb, mh)], tag="fst")


def ffn(e, final):
    old_lo = e.scr_lo
    e.scr_lo = 0
    prot = [0]
    banks4 = [(e.pool[i][:], ("pool", i)) for i in range(4)]

    def nxt_pool():
        prot[0] = (prot[0] + 1) % 4
        ap, key = banks4[prot[0]]
        return prot[0], ap, key

    base = e.p.scope
    e.p.scope = base + "_xq"
    ffn_xquant(e, 0)
    for mac in range(8):
        e.p.scope = base + "_al"
        ffn_alpha(e, mac, nxt_pool)
        e.p.scope = base + "_be"
        if mac + 1 < 8:
            h0 = {6: (lambda mac=mac: ffn_xquant_tb(e, mac + 1, 0)),
                  12: (lambda mac=mac: ffn_xquant_tb(e, mac + 1, 1)),
                  18: (lambda mac=mac: ffn_xquant_tb(e, mac + 1, 2))}
            h1 = {6: (lambda mac=mac: ffn_xquant_tb(e, mac + 1, 3))}
        else:
            h0 = h1 = None
        ffn_beta_mh(e, mac, 0, final, nxt_pool, hooks=h0)
        ffn_beta_mh(e, mac, 1, final, nxt_pool, hooks=h1)
    e.scr_lo = old_lo



def attn_b(e):
    prot = [0]

    def nxt_pool():
        prot[0] = (prot[0] + 1) % 4
        return prot[0], e.pool[prot[0]], ("pool", prot[0])

    for seq in range(2):
        e.p.scope = "attnB_pA"
        e.V(lambda v: v.memset(e.acc[:], 0.0),
            w=[("accA",), ("accB",)], tag="bacz")
        ln_apply(e, seq * BPS, e.lnt[:, 0:1024], [("lnt", 0), ("lnt", 1)],
                 tag="blna")
        for bt in range(BPS):
            gb = seq * BPS + bt
            buf = bt % 2
            lnt_ap = e.lnt[:, buf * 1024:(buf + 1) * 1024]
            if buf == 0:
                lnf_ap = e.lnf[:, 0:1024]
                lnf_keys = [("lnf", 0), ("lnf", 1)]
                lnf_rd = lambda kc: ("lnf", kc // 4)
            else:
                lnf_ap = e.sbd[:, 0:1024]
                lnf_keys = [("sbd", i) for i in range(8)]
                lnf_rd = lambda kc: ("sbd", kc)
            tp, tpn = (e.tpA, "tpA") if buf == 0 else (e.tpB, "tpB")
            transpose_1024(e, lnt_ap, [("lnt", 2 * buf), ("lnt", 2 * buf + 1)],
                           lnf_ap.rearrange("p (c x) -> p c x", x=128),
                           lnf_keys, tp, tpn, tag=f"blnT{gb}")
            if bt + 1 < BPS:
                nb = (bt + 1) % 2
                ln_apply(e, gb + 1, e.lnt[:, nb * 1024:(nb + 1) * 1024],
                         [("lnt", 2 * nb), ("lnt", 2 * nb + 1)], tag="blna")
            for h in range(2):
                bi, bank, bkey = nxt_pool()
                for kc in range(8):
                    e.T(lambda t, kc=kc, h=h, bank=bank, lnf_ap=lnf_ap: t.matmul(
                        bank[:], lnf_ap[:, kc * 128:(kc + 1) * 128],
                        e.wreg[:, 0 * 8192 + kc * 1024 + h * 512:
                               0 * 8192 + kc * 1024 + (h + 1) * 512],
                        start=(kc == 0), stop=(kc == 7)),
                        r=[lnf_rd(kc), ("wreg", 0, kc, h)], w=[bkey], tag="bk")
                sk, sk_k = scr_slot(e, 1)
                e.S(lambda s, bank=bank, sk=sk, bkey=bkey: s.activation(
                    sk, bank[:], AF.Silu),
                    r=[bkey, ("cpk",)], w=sk_k, tag="bksil")
                e.V(lambda v, sk=sk, h=h: v.tensor_scalar(
                    e.q2t[:, h * 512:(h + 1) * 512], sk, 1.0, None, ALU.add),
                    r=sk_k, w=[("q2t", h)], tag="bkphi")
            for h in range(2):
                bi, bank, bkey = nxt_pool()
                for kc in range(8):
                    e.T(lambda t, kc=kc, h=h, bank=bank, lnf_ap=lnf_ap: t.matmul(
                        bank[:], lnf_ap[:, kc * 128:(kc + 1) * 128],
                        e.wreg[:, 1 * 8192 + kc * 1024 + h * 512:
                               1 * 8192 + kc * 1024 + (h + 1) * 512],
                        start=(kc == 0), stop=(kc == 7)),
                        r=[lnf_rd(kc), ("wreg", 1, kc, h)], w=[bkey], tag="bv")
                e.V(lambda v, bank=bank, h=h, bkey=bkey: v.scalar_tensor_tensor(
                    e.k2t[:, h * 512:(h + 1) * 512], bank[:], 0.0,
                    e.btile[:, 0 * 1024 + h * 512:0 * 1024 + (h + 1) * 512],
                    ALU.bypass, ALU.add),
                    r=[bkey, ("bt", 0, h)], w=[("k2t", h)], tag="bvb")
            for g in range(8):
                e.T(lambda t, g=g, bt=bt: t.matmul(
                    e.acc[:, g * 128:(g + 1) * 128],
                    e.q2t[:, g * 128:(g + 1) * 128],
                    e.k2t[:, g * 128:(g + 1) * 128],
                    start=False, stop=(bt == BPS - 1), skip_group_check=True),
                    r=[("q2t", g // 4), ("k2t", g // 4)],
                    w=[("accA",) if g < 4 else ("accB",)], tag="bkv")
        e.V(lambda v: v.memset(e.sbd[:], 0.0),
            w=[("sbd", g) for g in range(8)], tag="bsbz")
        for g in range(8):
            e.S(lambda s, g=g: s.copy(
                e.sbd[0:64, g * 128:g * 128 + 64],
                e.acc[0:64, g * 128:g * 128 + 64]),
                r=[("accA",) if g < 4 else ("accB",), ("sbd", g)], w=[("sbd", g)], tag="bev0")
            e.S(lambda s, g=g: s.copy(
                e.sbd[64:128, g * 128 + 64:g * 128 + 128],
                e.acc[64:128, g * 128 + 64:g * 128 + 128]),
                r=[("accA",) if g < 4 else ("accB",), ("sbd", g)], w=[("sbd", g)], tag="bev1")
        e.p.scope = "attnB_pB"
        for sp in range(NSPAN):
            for mb in range(4):
                gb = seq * BPS + sp * 4 + mb
                buf = mb % 2
                lnt_ap = e.lnt[:, buf * 1024:(buf + 1) * 1024]
                ln_apply(e, gb, lnt_ap, [("lnt", 2 * buf), ("lnt", 2 * buf + 1)], tag="blnb")
                tp, tpn = (e.tpA, "tpA") if buf == 0 else (e.tpB, "tpB")
                dst3d = e.wbuf[:, 0:4096].rearrange(
                    "p (c t) -> p c t", c=8)[:, :, mb * 128:(mb + 1) * 128]
                transpose_1024(e, lnt_ap, [("lnt", 2 * buf), ("lnt", 2 * buf + 1)], dst3d,
                               wbk(0, 4096), tp, tpn, tag=f"bl3T{gb}")
            for ic in range(8):
                bi, bank, bkey = nxt_pool()
                for jc in range(8):
                    e.T(lambda t, ic=ic, jc=jc, bank=bank: t.matmul(
                        bank[:],
                        e.wreg[:, 2 * 8192 + jc * 1024 + ic * 128:
                               2 * 8192 + jc * 1024 + (ic + 1) * 128],
                        e.wbuf[:, jc * 512:(jc + 1) * 512],
                        start=(jc == 0), stop=(jc == 7)),
                        r=wbk(jc * 512, (jc + 1) * 512) +
                        [("wreg", 2, jc, ic // 4)],
                        w=[bkey], tag="bq")
                sq, sq_k = scr_slot(e, 1)
                e.S(lambda s, bank=bank, sq=sq, ic=ic, bkey=bkey: s.activation(
                    sq, bank[:], AF.Silu, bias=e.smf[:, 336 + ic:337 + ic],
                    scale=1.0),
                    r=[bkey, ("bqcol",), ("cpk",)], w=sq_k, tag="bqsil")
                e.V(lambda v, sq=sq, ic=ic: v.tensor_scalar(
                    e.vprime[:, 8192 + ic * 512:8192 + (ic + 1) * 512],
                    sq, 1.0, None, ALU.add),
                    r=sq_k, w=[("vpk", 16 + ic)], tag="bqphi")
            for g in range(8):
                bi, bank, bkey = nxt_pool()
                e.T(lambda t, g=g, bank=bank: t.matmul(
                    bank[:], e.sbd[:, g * 128:(g + 1) * 128],
                    e.vprime[:, 8192 + g * 512:8192 + (g + 1) * 512],
                    start=True, stop=True),
                    r=[("sbd", g), ("vpk", 16 + g)], w=[bkey], tag="bo")
                e.S(lambda s, g=g, bank=bank, bkey=bkey: s.copy(
                    e.vprime[:, 12288 + g * 512:12288 + (g + 1) * 512], bank[:]),
                    r=[bkey], w=[("vpk", 24 + g)], tag="boc")
            for mb in range(4):
                gb = seq * BPS + sp * 4 + mb
                for h in range(2):
                    bi, bank, bkey = nxt_pool()
                    for ec in range(8):
                        e.T(lambda t, ec=ec, h=h, mb=mb, bank=bank: t.matmul(
                            bank[:],
                            e.vprime[:, 12288 + ec * 512 + mb * 128:
                                     12288 + ec * 512 + (mb + 1) * 128],
                            e.vprime[:, ec * 1024 + h * 512:
                                     ec * 1024 + (h + 1) * 512],
                            start=(ec == 0), stop=(ec == 7)),
                            r=[("vpk", 24 + ec)] +
                            vpk(ec * 1024 + h * 512, ec * 1024 + (h + 1) * 512),
                            w=[bkey], tag="bwo")
                    o_ap, o_k = scr_slot(e, 1)
                    e.V(lambda v, bank=bank, h=h, o_ap=o_ap, bkey=bkey:
                        v.scalar_tensor_tensor(
                            o_ap, bank[:], 0.0,
                            e.btile[:, 1 * 1024 + h * 512:1 * 1024 + (h + 1) * 512],
                            ALU.bypass, ALU.add),
                        r=[bkey, ("bt", 1, h)], w=o_k, tag="bob")
                    e.P(lambda v, o_ap=o_ap, gb=gb, h=h: v.tensor_tensor(
                        e.stream[:, gb * 1024 + h * 512:gb * 1024 + (h + 1) * 512],
                        o_ap,
                        e.stream[:, gb * 1024 + h * 512:gb * 1024 + (h + 1) * 512],
                        ALU.add),
                        r=o_k + [("st", gb)], w=[("st", gb)], tag="bres")
                    if h == 1:
                        stats_block(e, gb)



_WNAMES = [
    "L0_attn_a_Wq", "L0_attn_a_bq", "L0_attn_a_Wk", "L0_attn_a_bk",
    "L0_attn_a_Wv", "L0_attn_a_bv", "L0_attn_a_Wq2", "L0_attn_a_bq2",
    "L0_attn_a_Wk2", "L0_attn_a_bk2", "L0_attn_a_Wv2", "L0_attn_a_bv2",
    "L0_attn_a_Wo", "L0_attn_a_bo", "L0_attn_a_gamma", "L0_attn_a_beta",
    "L0_attn_b_Wq", "L0_attn_b_bq", "L0_attn_b_Wk", "L0_attn_b_Wv",
    "L0_attn_b_bv", "L0_attn_b_Wo", "L0_attn_b_bo",
    "L0_ffn_w_gate", "L0_ffn_w_up", "L0_ffn_w_down",
]


def kernel(**inputs):
    x = np.asarray(inputs["L0_attn_a_x"], dtype=np.float32)
    nc = build_program()
    in_maps = []
    for c in range(8):
        m = {"xin": np.ascontiguousarray(x[2 * c:2 * c + 2].reshape(NBLK * 128, D))}
        for n in _WNAMES:
            m[n] = np.ascontiguousarray(inputs[n], dtype=np.float32)
        in_maps.append(m)
    res = run_bass_kernel_spmd(nc, in_maps, core_ids=list(range(8)))
    out = np.concatenate(
        [res.results[c]["out"].reshape(2, 2048, D) for c in range(8)], axis=0)
    return out


if __name__ == "__main__":
    import os
    os.environ.setdefault("KDEBUG", "1")
    nc = build_program()
    print("build ok")
